# version 37
# baseline (speedup 1.0000x reference)
import numpy as np
import bass_rust
import concourse.bass as bass
import concourse.mybir as mybir
from concourse.bass_utils import run_bass_kernel_spmd

F32 = mybir.dt.float32
I32 = mybir.dt.int32
BF16 = mybir.dt.bfloat16
AF = mybir.ActivationFunctionType
OP = mybir.AluOpType
AX = mybir.AxisListType

S = 4096
E = 1024
H = 16
D = 64
NT = S // 128
NE = E // 128
C75I = float(np.float32(1.0 / 7.5))
INV1024 = float(np.float32(1.0 / 1024.0))

WB_HI = 0
WB_LO = 4224
MIXH = 8448
MIXL = 10496
XHP = 8448
XLP = 9472
TPST = (12544, 13568)
WPSP = 14592
MTH = (0, 1024)
MTL = (2048, 3072)
XH = 4096
XL = 5120
XDTH = 6144
XDTL = 7168

SM_MEAN, SM_VACC, SM_VAR, SM_SQRT, SM_INV = 0, 1, 2, 3, 4
SM_ABS, SM_SC, SM_INV8, SM_SC8 = 8, 40, 72, 104
SM_MNEG, SM_SA, SM_SB, SM_S, SM_R = 136, 137, 138, 139, 140


class Ctr:
    def __init__(self):
        self.n = 0

    def inc(self, k=1):
        self.n += k
        return self.n


def build_nc(debug_taps=False):
    nc = bass.Bass("TRN2", target_bir_lowering=False, detect_race_conditions=False)

    x_d = nc.dram_tensor("x", [S, E], F32, kind="ExternalInput").ap()
    wa_d = nc.dram_tensor("wa", [H, E], F32, kind="ExternalInput").ap()
    wv_d = nc.dram_tensor("wv", [E, E], F32, kind="ExternalInput").ap()
    wo_d = nc.dram_tensor("wo", [E, E], F32, kind="ExternalInput").ap()
    bo_d = nc.dram_tensor("bo", [E], F32, kind="ExternalInput").ap()
    fw_d = nc.dram_tensor("fw", [E, E], F32, kind="ExternalInput").ap()
    fb_d = nc.dram_tensor("fb", [E], F32, kind="ExternalInput").ap()
    y_d = nc.dram_tensor("y", [S, E], F32, kind="ExternalOutput").ap()

    if debug_taps:
        dbg_res = nc.dram_tensor("dbg_res", [128, 32768], BF16, kind="ExternalOutput").ap()
        dbg_th = nc.dram_tensor("dbg_th", [128, 128], BF16, kind="ExternalOutput").ap()
        dbg_tl = nc.dram_tensor("dbg_tl", [128, 128], BF16, kind="ExternalOutput").ap()
        dbg_mh = nc.dram_tensor("dbg_mh", [128, 2048], BF16, kind="ExternalOutput").ap()
        dbg_ml = nc.dram_tensor("dbg_ml", [128, 2048], BF16, kind="ExternalOutput").ap()
        dbg_z = nc.dram_tensor("dbg_z", [H, S], F32, kind="ExternalOutput").ap()
        dbg_wh = nc.dram_tensor("dbg_wh", [H, 4224], BF16, kind="ExternalOutput").ap()
        dbg_wl = nc.dram_tensor("dbg_wl", [H, 4224], BF16, kind="ExternalOutput").ap()
        dbg_vh = nc.dram_tensor("dbg_vh", [128, 32768], BF16, kind="ExternalOutput").ap()
        dbg_vl = nc.dram_tensor("dbg_vl", [128, 32768], BF16, kind="ExternalOutput").ap()
    wot_h_d = nc.dram_tensor("wot_h", [NE, 128, E], BF16, kind="Internal").ap()
    wot_l_d = nc.dram_tensor("wot_l", [NE, 128, E], BF16, kind="Internal").ap()
    wdt_h_d = nc.dram_tensor("wdt_h", [NE, 128, E], BF16, kind="Internal").ap()
    wdt_l_d = nc.dram_tensor("wdt_l", [NE, 128, E], BF16, kind="Internal").ap()
    mxt_h_d = nc.dram_tensor("mxt_h", [32 * 128, NE * 128], BF16, kind="Internal").ap()
    mxt_l_d = nc.dram_tensor("mxt_l", [32 * 128, NE * 128], BF16, kind="Internal").ap()

    arena_v = nc.alloc_sbuf_tensor("arena_v", [128, 65536], BF16)
    arena_w = nc.alloc_sbuf_tensor("arena_w", [128, 16384], BF16)
    pool = nc.alloc_sbuf_tensor("pool", [128, 8192], F32)
    xt_h = nc.alloc_sbuf_tensor("xt_h", [128, 1024], BF16)
    xt_l = nc.alloc_sbuf_tensor("xt_l", [128, 1024], BF16)
    ibuf = nc.alloc_sbuf_tensor("ibuf", [128, 1024], I32)
    smalls = nc.alloc_sbuf_tensor("smalls", [128, 256], F32)
    th_bufs = [nc.alloc_sbuf_tensor(f"th_{i}", [128, 512], BF16) for i in range(2)]
    tl_bufs = [nc.alloc_sbuf_tensor(f"tl_{i}", [128, 512], BF16) for i in range(2)]
    wat_h = nc.alloc_sbuf_tensor("wat_h", [128, 128], BF16)
    wat_l = nc.alloc_sbuf_tensor("wat_l", [128, 128], BF16)
    idb = nc.alloc_sbuf_tensor("idb", [128, 128], BF16)
    ones_bf = nc.alloc_sbuf_tensor("ones_bf", [1, 128], BF16)

    ps_a = nc.alloc_psum_tensor("ps_a", [128, 1024], F32)
    ps_b = nc.alloc_psum_tensor("ps_b", [128, 1024], F32)
    ps_c = nc.alloc_psum_tensor("ps_c", [128, 1024], F32)
    ps_tp = nc.alloc_psum_tensor("ps_tp", [128, 2048], BF16)

    POOL = pool.ap()
    AV = arena_v.ap()
    AW = arena_w.ap()
    SM = smalls.ap()

    def pslot(i, r0=0, r1=128):
        return POOL[r0:r1, i * 1024:(i + 1) * 1024]

    def aw_ap(offset, dims):
        return bass_rust.AP(tensor=arena_w.ap().tensor, offset=offset, ap=dims)

    def av_ap(offset, dims):
        return bass_rust.AP(tensor=arena_v.ap().tensor, offset=offset, ap=dims)

    ops = {k: [] for k in ("sync", "gpsimd", "vector", "scalar", "tensor")}

    def em(engine, fn):
        ops[engine].append(fn)

    import contextlib
    es = contextlib.ExitStack()
    with es:
        pe = es.enter_context(nc.semaphore("PE"))
        dv = es.enter_context(nc.semaphore("DV"))
        ac = es.enter_context(nc.semaphore("AC"))
        gq = es.enter_context(nc.semaphore("GQ"))
        sm_in = es.enter_context(nc.semaphore("sm_in"))
        sm_wv0 = es.enter_context(nc.semaphore("sm_wv0"))
        sm_wv1 = es.enter_context(nc.semaphore("sm_wv1"))
        sm_x0 = es.enter_context(nc.semaphore("sm_x0"))
        sm_x1 = es.enter_context(nc.semaphore("sm_x1"))
        sm_wp = es.enter_context(nc.semaphore("sm_wp"))
        sm_sp0 = es.enter_context(nc.semaphore("sm_sp0"))
        sm_sp1 = es.enter_context(nc.semaphore("sm_sp1"))
        sm_mt0 = es.enter_context(nc.semaphore("sm_mt0"))
        sm_mt1 = es.enter_context(nc.semaphore("sm_mt1"))
        sm_wres = es.enter_context(nc.semaphore("sm_wres"))
        sm_t0 = es.enter_context(nc.semaphore("sm_t0"))
        sm_t1 = es.enter_context(nc.semaphore("sm_t1"))
        sm_t2 = es.enter_context(nc.semaphore("sm_t2"))
        sm_t3 = es.enter_context(nc.semaphore("sm_t3"))
        sm_out = es.enter_context(nc.semaphore("sm_out"))
        block = es.enter_context(nc.Block())

        sm_wv = (sm_wv0, sm_wv1)
        sm_x = (sm_x0, sm_x1)
        sm_sp = (sm_sp0, sm_sp1)
        sm_mt = (sm_mt0, sm_mt1)
        sm_t = (sm_t0, sm_t1, sm_t2, sm_t3)
        PE, DV, AC, GQ = Ctr(), Ctr(), Ctr(), Ctr()
        C_WV = [Ctr(), Ctr()]
        C_X = [Ctr(), Ctr()]
        C_WP = Ctr()
        C_SP = [Ctr(), Ctr()]
        C_MT = [Ctr(), Ctr()]
        C_T = [Ctr(), Ctr(), Ctr(), Ctr()]
        C_OUT = Ctr()

        def emit_quant(in_ap, out_xh_ap, out_xl_ap, pre_sq_wait=None):
            s0, s1, s2, s3 = pslot(0), pslot(1), pslot(2), pslot(3)
            in3 = in_ap.rearrange("p (b k) -> p b k", b=32)
            s0_3 = s0.rearrange("p (b k) -> p b k", b=32)

            def _(v):
                if pre_sq_wait is not None:
                    v.wait_ge(sm_out, pre_sq_wait)
                v.tensor_reduce(out=SM[:, SM_ABS:SM_ABS + 32], in_=in3, axis=AX.X,
                                op=OP.max, apply_absolute_value=True)
                v.drain()
                v.tensor_scalar(out=SM[:, SM_SC:SM_SC + 32], in0=SM[:, SM_ABS:SM_ABS + 32],
                                scalar1=C75I, scalar2=1e-30, op0=OP.mult, op1=OP.max)
                v.drain()
                v.reciprocal(out=SM[:, SM_INV8:SM_INV8 + 32], in_=SM[:, SM_SC:SM_SC + 32])
                v.drain()
                v.tensor_scalar(out=SM[:, SM_INV8:SM_INV8 + 32], in0=SM[:, SM_INV8:SM_INV8 + 32],
                                scalar1=8.0, scalar2=None, op0=OP.mult)
                v.tensor_scalar(out=SM[:, SM_SC8:SM_SC8 + 32], in0=SM[:, SM_SC:SM_SC + 32],
                                scalar1=0.125, scalar2=None, op0=OP.mult)
                v.drain()
                inv8bc = SM[:, SM_INV8:SM_INV8 + 32].unsqueeze(2).broadcast_to([128, 32, 32])
                v.tensor_tensor(out=s0_3, in0=in3, in1=inv8bc, op=OP.mult)
                v.drain().then_inc(dv, 1)
            em("vector", _)
            d1 = DV.inc(1)
            emit_quant.last_d1 = d1

            def _(a):
                a.wait_ge(dv, d1)
                a.activation(out=s1, in_=s0, func=AF.Abs)
                a.drain().then_inc(ac, 1)
            em("scalar", _)
            a1 = AC.inc(1)

            def _(v):
                v.wait_ge(ac, a1)
                v.tensor_scalar(out=s2, in0=s1, scalar1=15.5, scalar2=0.5, op0=OP.is_le, op1=OP.max)
                v.tensor_scalar(out=s3, in0=s1, scalar1=31.0, scalar2=0.5, op0=OP.is_le, op1=OP.max)
                v.tensor_tensor(out=s1, in0=s0, in1=s2, op=OP.mult)
                v.tensor_tensor(out=s0, in0=s1, in1=s3, op=OP.mult)
                v.drain().then_inc(dv, 1)
            em("vector", _)
            d2 = DV.inc(1)

            def _(a):
                a.wait_ge(dv, d2)
                a.activation(out=s1, in_=s0, func=AF.Copy, bias=0.5, scale=-1.0)
                a.drain().then_inc(ac, 1)
            em("scalar", _)
            a2 = AC.inc(1)

            def _(v):
                v.wait_ge(ac, a2)
                v.tensor_scalar(out=ibuf.ap()[:, 0:512], in0=s1[:, 0:512],
                                scalar1=0.0, scalar2=None, op0=OP.add)
                v.drain().then_inc(dv, 1)
            em("vector", _)
            d3a = DV.inc(1)

            def _(a):
                a.wait_ge(dv, d3a)
                a.copy(out=s0[:, 0:512], in_=ibuf.ap()[:, 0:512])
                a.drain().then_inc(ac, 1)
            em("scalar", _)
            a3a = AC.inc(1)

            def _(v):
                v.wait_ge(ac, a3a)
                v.tensor_scalar(out=ibuf.ap()[:, 0:512], in0=s1[:, 512:1024],
                                scalar1=0.0, scalar2=None, op0=OP.add)
                v.drain().then_inc(dv, 1)
            em("vector", _)
            d3b = DV.inc(1)

            def _(a):
                a.wait_ge(dv, d3b)
                a.copy(out=s0[:, 512:1024], in_=ibuf.ap()[:, 0:512])
                a.drain().then_inc(ac, 1)
            em("scalar", _)
            a3b = AC.inc(1)

            def _(v):
                v.wait_ge(ac, a3b)
                v.tensor_scalar(out=s2, in0=s2, scalar1=-4.0, scalar2=6.0, op0=OP.mult, op1=OP.add)
                v.tensor_scalar(out=s3, in0=s3, scalar1=-1.0, scalar2=1.5, op0=OP.mult, op1=OP.add)
                v.tensor_tensor(out=s2, in0=s2, in1=s3, op=OP.mult)
                v.tensor_tensor(out=s3, in0=s0, in1=s1, op=OP.is_gt)
                v.tensor_tensor(out=s0, in0=s3, in1=s0, op=OP.subtract)
                v.tensor_tensor(out=s0, in0=s0, in1=s2, op=OP.mult)
                sc8bc = SM[:, SM_SC8:SM_SC8 + 32].unsqueeze(2).broadcast_to([128, 32, 32])
                v.tensor_tensor(out=s0_3, in0=s0_3, in1=sc8bc, op=OP.mult)
                v.drain().then_inc(dv, 1)
            em("vector", _)
            d4 = DV.inc(1)

            def _(a):
                a.wait_ge(dv, d4)
                a.copy(out=out_xh_ap, in_=s0)
                a.drain().then_inc(ac, 1)
            em("scalar", _)
            a4 = AC.inc(1)

            def _(v):
                v.wait_ge(ac, a4)
                v.tensor_tensor(out=out_xl_ap, in0=s0, in1=out_xh_ap, op=OP.subtract)
                v.drain().then_inc(dv, 1)
            em("vector", _)
            d5 = DV.inc(1)
            return d5, a4

        def _(s):
            s.dma_start(out=POOL[0:H, 0:1024], in_=wa_d[:, :]).then_inc(sm_in, 16)
        em("sync", _)
        sq_wa = 16

        def _(g):
            g.memset(idb.ap(), 0.0).then_inc(gq, 1)
            g.memset(ones_bf.ap(), 1.0).then_inc(gq, 1)
            g.memset(idb.ap()[0:1, 0:1], 0.0).then_inc(gq, 1)
            g.memset(idb.ap()[0:1, 0:1], 0.0).then_inc(gq, 1)
            g.wait_ge(gq, 4)
            g.affine_select(out=idb.ap(), in_=idb.ap(), compare_op=OP.not_equal,
                            fill=1.0, base=0, pattern=[[-1, 128]], channel_multiplier=1).then_inc(gq, 1)
            g.memset(SM[:, 240:241], 1e-5).then_inc(gq, 1)
            g.memset(SM[:, 241:242], INV1024).then_inc(gq, 1)
            g.memset(SM[:, 242:243], -INV1024).then_inc(gq, 1)
        em("gpsimd", _)
        gq_setup = GQ.inc(8)
        nc.const_aps.register(F32, 1e-5, SM[:, 240:241])
        nc.const_aps.register(F32, INV1024, SM[:, 241:242])
        nc.const_aps.register(F32, -INV1024, SM[:, 242:243])

        def _(v):
            v.wait_ge(sm_in, sq_wa)
            v.tensor_scalar(out=POOL[0:H, 3072:4096], in0=POOL[0:H, 0:1024],
                            scalar1=0.125, scalar2=None, op0=OP.mult)
            v.drain().then_inc(dv, 1)
        em("vector", _)
        dv_was = DV.inc(1)

        def _(a):
            a.wait_ge(dv, dv_was)
            a.copy(out=xt_h.ap()[0:H, 0:1024], in_=POOL[0:H, 3072:4096]).then_inc(ac, 1)
        em("scalar", _)
        ac_w0 = AC.inc(1)

        def _(v):
            v.wait_ge(ac, ac_w0)
            v.tensor_tensor(out=xt_l.ap()[0:H, 0:1024], in0=POOL[0:H, 3072:4096],
                            in1=xt_h.ap()[0:H, 0:1024], op=OP.subtract).then_inc(dv, 1)
        em("vector", _)
        dv_w0 = DV.inc(1)

        def _(t):
            t.wait_ge(gq, gq_setup)
            t.wait_ge(dv, dv_w0)
            ins = None
            for Ei in range(NE):
                ins = t.transpose(out=ps_tp.ap()[:, Ei * 16:(Ei + 1) * 16],
                                  in_=xt_h.ap()[0:H, Ei * 128:(Ei + 1) * 128],
                                  identity=idb.ap()[0:H, 0:H])
            for Ei in range(NE):
                ins = t.transpose(out=ps_tp.ap()[:, 128 + Ei * 16:128 + (Ei + 1) * 16],
                                  in_=xt_l.ap()[0:H, Ei * 128:(Ei + 1) * 128],
                                  identity=idb.ap()[0:H, 0:H])
            ins.then_inc(pe, 1)
        em("tensor", _)
        pe_wat = PE.inc(1)

        def _(v):
            v.wait_ge(pe, pe_wat)
            v.tensor_copy(out=wat_h.ap()[:, 0:128], in_=ps_tp.ap()[:, 0:128])
            v.tensor_copy(out=wat_l.ap()[:, 0:128], in_=ps_tp.ap()[:, 128:256])
            v.drain().then_inc(dv, 1)
        em("vector", _)
        dv_wat = DV.inc(1)

        sq_wv = [0] * NE
        ac_wvs = [0] * NE
        dv_wvl = [0] * NE
        pe_wvt = [0] * NE
        dv_wve = [0] * NE
        wvbh = (xt_h.ap()[:, :], pslot(6).bitcast(BF16)[:, 0:1024])
        wvbl = (xt_l.ap()[:, :], pslot(7).bitcast(BF16)[:, 0:1024])

        def emit_wve(Fi):
            def _(v, Fi=Fi):
                v.wait_ge(pe, pe_wvt[Fi])
                dst_h = aw_ap(Fi * 128, [[16384, 128], [1024, NE], [1, 128]])
                v.tensor_copy(out=dst_h, in_=ps_tp.ap()[:, 0:1024].rearrange("p (e f) -> p e f", e=NE))
                dst_l = aw_ap(8192 + Fi * 128, [[16384, 128], [1024, NE], [1, 128]])
                v.tensor_copy(out=dst_l, in_=ps_tp.ap()[:, 1024:2048].rearrange("p (e f) -> p e f", e=NE)).then_inc(dv, 1)
            em("vector", _)
            dv_wve[Fi] = DV.inc(1)

        for Fi in range(NE):
            slot = 4 + (Fi % 2)
            WH = wvbh[Fi % 2]
            WL = wvbl[Fi % 2]

            def _(s, Fi=Fi, slot=slot):
                if Fi >= 2:
                    s.wait_ge(dv, dv_wvl[Fi - 2])
                s.dma_start(out=pslot(slot), in_=wv_d[Fi * 128:(Fi + 1) * 128, :]).then_inc(sm_wv[Fi % 2], 16)
            em("sync", _)
            sq_wv[Fi] = C_WV[Fi % 2].inc(16)

            def _(a, Fi=Fi, slot=slot, WH=WH):
                a.wait_ge(sm_wv[Fi % 2], sq_wv[Fi])
                if Fi >= 2:
                    a.wait_ge(pe, pe_wvt[Fi - 2])
                elif Fi == 0:
                    a.wait_ge(pe, pe_wat)
                a.copy(out=WH, in_=pslot(slot)).then_inc(ac, 1)
            em("scalar", _)
            ac_wvs[Fi] = AC.inc(1)

            def _(v, Fi=Fi, slot=slot, WH=WH, WL=WL):
                v.wait_ge(ac, ac_wvs[Fi])
                v.tensor_tensor(out=WL, in0=pslot(slot), in1=WH, op=OP.subtract).then_inc(dv, 1)
            em("vector", _)
            dv_wvl[Fi] = DV.inc(1)

            if Fi >= 1:
                emit_wve(Fi - 1)

            def _(t, Fi=Fi, WH=WH, WL=WL):
                t.wait_ge(dv, dv_wvl[Fi])
                if Fi >= 1:
                    t.wait_ge(dv, dv_wve[Fi - 1])
                ins = None
                for Ei in range(NE):
                    ins = t.transpose(out=ps_tp.ap()[:, Ei * 128:(Ei + 1) * 128],
                                      in_=WH[:, Ei * 128:(Ei + 1) * 128], identity=idb.ap())
                for Ei in range(NE):
                    ins = t.transpose(out=ps_tp.ap()[:, 1024 + Ei * 128:1024 + (Ei + 1) * 128],
                                      in_=WL[:, Ei * 128:(Ei + 1) * 128], identity=idb.ap())
                ins.then_inc(pe, 1)
            em("tensor", _)
            pe_wvt[Fi] = PE.inc(1)
        emit_wve(NE - 1)

        sq_x = [0] * NT
        ac_xhp = [0] * NT
        ac_xte = [0] * NT
        dv_xlp = [0] * NT
        pe_xtp = [0] * NT
        dv_xte = [0] * NT
        pe_zmm = [0] * NT
        pe_vmm = [0] * NT
        dv_zev = [0] * NT
        ac_vhi = [0] * NT
        dv_vlo = [0] * NT

        xbh = (xt_h.ap()[:, :], pslot(6).bitcast(BF16)[:, 0:1024])
        xbl = (xt_l.ap()[:, :], pslot(7).bitcast(BF16)[:, 0:1024])

        def emit_vevac(T):
            vps = ps_b if (T % 2 == 0) else ps_c

            def _(a, T=T, vps=vps):
                a.wait_ge(pe, pe_vmm[T])
                dst = av_ap(T * 64, [[65536, 128], [2048, H], [1, 64]])
                a.copy(out=dst, in_=vps.ap().rearrange("p (h d) -> p h d", h=H)).then_inc(ac, 1)
            em("scalar", _)
            ac_vhi[T] = AC.inc(1)

            def _(v, T=T, vps=vps):
                v.wait_ge(ac, ac_vhi[T])
                hi = av_ap(T * 64, [[65536, 128], [2048, H], [1, 64]])
                lo = av_ap(32768 + T * 64, [[65536, 128], [2048, H], [1, 64]])
                v.tensor_tensor(out=lo, in0=vps.ap().rearrange("p (h d) -> p h d", h=H),
                                in1=hi, op=OP.subtract).then_inc(dv, 1)
            em("vector", _)
            dv_vlo[T] = DV.inc(1)

        for T in range(NT):
            slot = 4 + (T % 2)
            XH = xbh[T % 2]
            XL = xbl[T % 2]

            def _(s, T=T, slot=slot):
                if T >= 2:
                    s.wait_ge(dv, dv_xlp[T - 2])
                else:
                    s.wait_ge(dv, dv_wvl[NE - 1])
                s.dma_start(out=pslot(slot), in_=x_d[T * 128:(T + 1) * 128, :]).then_inc(sm_x[T % 2], 16)
            em("sync", _)
            sq_x[T] = C_X[T % 2].inc(16)

            def _(a, T=T, slot=slot, XH=XH):
                a.wait_ge(sm_x[T % 2], sq_x[T])
                if T >= 2:
                    a.wait_ge(pe, pe_vmm[T - 2])
                else:
                    a.wait_ge(pe, pe_wvt[NE - 1])
                a.copy(out=XH, in_=pslot(slot)).then_inc(ac, 1)
            em("scalar", _)
            ac_xhp[T] = AC.inc(1)

            def _(v, T=T, slot=slot, XH=XH, XL=XL):
                v.wait_ge(ac, ac_xhp[T])
                v.tensor_tensor(out=XL, in0=pslot(slot), in1=XH, op=OP.subtract).then_inc(dv, 1)
            em("vector", _)
            dv_xlp[T] = DV.inc(1)

            if T >= 1:
                emit_vevac(T - 1)

            def _(t, T=T, XH=XH, XL=XL):
                t.wait_ge(dv, dv_xlp[T])
                if T == 0:
                    t.wait_ge(dv, dv_wve[NE - 1])
                else:
                    t.wait_ge(dv, dv_xte[T - 1])
                    t.wait_ge(ac, ac_xte[T - 1])
                ins = None
                for Ei in range(NE):
                    ins = t.transpose(out=ps_tp.ap()[:, Ei * 128:(Ei + 1) * 128],
                                      in_=XH[:, Ei * 128:(Ei + 1) * 128], identity=idb.ap())
                for Ei in range(NE):
                    ins = t.transpose(out=ps_tp.ap()[:, 1024 + Ei * 128:1024 + (Ei + 1) * 128],
                                      in_=XL[:, Ei * 128:(Ei + 1) * 128], identity=idb.ap())
                ins.then_inc(pe, 1)
            em("tensor", _)
            pe_xtp[T] = PE.inc(1)

            def _(a, T=T, XH=XH):
                a.wait_ge(pe, pe_xtp[T])
                a.copy(out=XH, in_=ps_tp.ap()[:, 0:1024]).then_inc(ac, 1)
            em("scalar", _)
            ac_xte[T] = AC.inc(1)

            def _(v, T=T, XL=XL):
                v.wait_ge(pe, pe_xtp[T])
                v.tensor_copy(out=XL, in_=ps_tp.ap()[:, 1024:2048]).then_inc(dv, 1)
            em("vector", _)
            dv_xte[T] = DV.inc(1)

            def _(t, T=T, XH=XH, XL=XL):
                t.wait_ge(dv, dv_xte[T])
                t.wait_ge(ac, ac_xte[T])
                if T >= 1:
                    if T >= 2:
                        t.wait_ge(dv, dv_zev[T - 2])
                if T == 0:
                    t.wait_ge(dv, dv_wat)
                ins = None
                first = True
                for Ei in range(NE):
                    for wt, xs in ((wat_h, XH), (wat_h, XL), (wat_l, XH)):
                        ins = t.matmul(out=ps_a.ap()[0:H, (T % 2) * 128:(T % 2) * 128 + 128],
                                       lhsT=wt.ap()[:, Ei * 16:(Ei + 1) * 16],
                                       rhs=xs[:, Ei * 128:(Ei + 1) * 128],
                                       start=first, stop=(Ei == NE - 1 and wt is wat_l and xs is XH))
                        first = False
                ins.then_inc(pe, 1)
            em("tensor", _)
            pe_zmm[T] = PE.inc(1)

            vps = ps_b if (T % 2 == 0) else ps_c

            def _(t, T=T, vps=vps, XH=XH, XL=XL):
                if T >= 2:
                    t.wait_ge(dv, dv_vlo[T - 2])
                ins = None
                for Ei in range(NE):
                    for pi, (xs, woff) in enumerate(((XH, 0), (XH, 8192), (XL, 0))):
                        for ch in range(2):
                            ins = t.matmul(
                                out=vps.ap()[:, ch * 512:(ch + 1) * 512],
                                lhsT=xs[:, Ei * 128:(Ei + 1) * 128],
                                rhs=AW[:, woff + Ei * 1024 + ch * 512: woff + Ei * 1024 + (ch + 1) * 512],
                                start=(Ei == 0 and pi == 0), stop=(Ei == NE - 1 and pi == 2))
                ins.then_inc(pe, 1)
            em("tensor", _)
            pe_vmm[T] = PE.inc(1)

            def _(v, T=T):
                v.wait_ge(pe, pe_zmm[T])
                v.tensor_copy(out=POOL[0:H, T * 128:(T + 1) * 128],
                              in_=ps_a.ap()[0:H, (T % 2) * 128:(T % 2) * 128 + 128]).then_inc(dv, 1)
            em("vector", _)
            dv_zev[T] = DV.inc(1)

        emit_vevac(NT - 1)

        def _(v):
            v.drain()
            v.reduce_max(out=SM[0:H, SM_MNEG:SM_MNEG + 1], in_=POOL[0:H, 0:4096],
                         axis=AX.X, negate=True)
            v.drain().then_inc(dv, 1)
        em("vector", _)
        dv_mneg = DV.inc(1)

        def _(a):
            a.wait_ge(dv, dv_mneg)
            a.activation(out=POOL[0:H, 4096:6144], in_=POOL[0:H, 0:2048], func=AF.Exp,
                         bias=SM[0:H, SM_MNEG:SM_MNEG + 1], scale=1.0).then_inc(ac, 1)
        em("scalar", _)
        ac_expA = AC.inc(1)

        def _(v):
            v.wait_ge(ac, ac_expA)
            v.reduce_sum(out=SM[0:H, SM_SA:SM_SA + 1], in_=POOL[0:H, 4096:6144], axis=AX.X)
            v.drain().then_inc(dv, 1)
        em("vector", _)
        dv_sA = DV.inc(1)

        def _(a):
            a.wait_ge(dv, dv_sA)
            a.activation(out=POOL[0:H, 4096:6144], in_=POOL[0:H, 2048:4096], func=AF.Exp,
                         bias=SM[0:H, SM_MNEG:SM_MNEG + 1], scale=1.0).then_inc(ac, 1)
        em("scalar", _)
        ac_exp = AC.inc(1)

        def _(v):
            v.wait_ge(ac, ac_exp)
            v.reduce_sum(out=SM[0:H, SM_SB:SM_SB + 1], in_=POOL[0:H, 4096:6144], axis=AX.X)
            v.drain()
            v.tensor_tensor(out=SM[0:H, SM_S:SM_S + 1], in0=SM[0:H, SM_SA:SM_SA + 1],
                            in1=SM[0:H, SM_SB:SM_SB + 1], op=OP.add)
            v.drain()
            v.reciprocal(out=SM[0:H, SM_R:SM_R + 1], in_=SM[0:H, SM_S:SM_S + 1])
            v.drain().then_inc(dv, 1)
        em("vector", _)
        dv_r = DV.inc(1)

        def _(a):
            a.wait_ge(dv, dv_r)
            a.wait_ge(gq, gq_setup)
            a.activation(out=AW[0:H, WB_HI + 128 + 2048: WB_HI + 128 + 4096],
                         in_=POOL[0:H, 4096:6144], func=AF.Copy,
                         scale=SM[0:H, SM_R:SM_R + 1]).then_inc(ac, 1)
        em("scalar", _)
        ac_whB = AC.inc(1)

        def _(v):
            v.wait_ge(ac, ac_whB)
            v.scalar_tensor_tensor(out=AW[0:H, WB_LO + 128 + 2048: WB_LO + 128 + 4096],
                                   in0=POOL[0:H, 4096:6144], scalar=SM[0:H, SM_R:SM_R + 1],
                                   in1=AW[0:H, WB_HI + 128 + 2048: WB_HI + 128 + 4096],
                                   op0=OP.mult, op1=OP.subtract)
            v.drain().then_inc(dv, 1)
        em("vector", _)
        dv_wlB = DV.inc(1)

        def _(a):
            a.wait_ge(dv, dv_wlB)
            a.activation(out=POOL[0:H, 4096:6144], in_=POOL[0:H, 0:2048], func=AF.Exp,
                         bias=SM[0:H, SM_MNEG:SM_MNEG + 1], scale=1.0)
            a.activation(out=AW[0:H, WB_HI + 128: WB_HI + 128 + 2048],
                         in_=POOL[0:H, 4096:6144], func=AF.Copy,
                         scale=SM[0:H, SM_R:SM_R + 1]).then_inc(ac, 1)
        em("scalar", _)
        ac_whA = AC.inc(1)

        def _(v):
            v.wait_ge(ac, ac_whA)
            v.scalar_tensor_tensor(out=AW[0:H, WB_LO + 128: WB_LO + 128 + 2048],
                                   in0=POOL[0:H, 4096:6144], scalar=SM[0:H, SM_R:SM_R + 1],
                                   in1=AW[0:H, WB_HI + 128: WB_HI + 128 + 2048],
                                   op0=OP.mult, op1=OP.subtract)
            v.drain().then_inc(dv, 1)
        em("vector", _)
        dv_wbufs = DV.inc(1)

        if debug_taps:
            def _(s):
                s.wait_ge(dv, dv_mneg)
                s.dma_start(out=dbg_z[:, :], in_=POOL[0:H, 0:4096]).then_inc(sm_in, 16)
                s.wait_ge(dv, dv_wbufs)
                s.dma_start(out=dbg_wh[:, :], in_=AW[0:H, WB_HI:WB_HI + 4224]).then_inc(sm_in, 16)
                s.dma_start(out=dbg_wl[:, :], in_=AW[0:H, WB_LO:WB_LO + 4224]).then_inc(sm_in, 16)
                s.dma_start(out=dbg_vh[:, :], in_=AV[:, 0:32768]).then_inc(sm_in, 16)
                s.dma_start(out=dbg_vl[:, :], in_=AV[:, 32768:65536]).then_inc(sm_in, 16)
            em("sync", _)

        def wt_res_ap(d):
            return bass_rust.AP(tensor=d.tensor, offset=0,
                                ap=[[E, 128], [128 * E, NE], [1, E]])

        wp_wo_total = [0]
        wprep_d1 = {}
        dv_wprep_last = [0] * (2 * NE)
        wprep_loads = {}

        wprep_sql = {}

        def emit_wprep_load_dma(idx):
            is_fw = idx >= NE
            Gi = idx - NE if is_fw else idx
            slot = 4 + (idx % 2)
            w_src = fw_d if is_fw else wo_d

            def _(s, idx=idx, Gi=Gi, slot=slot, w_src=w_src):
                if idx < 2:
                    s.wait_ge(dv, dv_wbufs)
                else:
                    s.wait_ge(dv, dv_wprep_last[idx - 2])
                s.dma_start(out=pslot(slot), in_=w_src[Gi * 128:(Gi + 1) * 128, :]).then_inc(sm_x[idx % 2], 16)
            em("sync", _)
            wprep_sql[idx] = C_X[idx % 2].inc(16)

        def emit_wprep_load(idx):
            is_fw = idx >= NE
            Gi = idx - NE if is_fw else idx
            slot = 4 + (idx % 2)
            sq_l = wprep_sql[idx]

            if not is_fw:
                def _(a, slot=slot, sq_l=sq_l, idx=idx, prev_pe=PE.n):
                    a.wait_ge(sm_x[idx % 2], sq_l)
                    a.wait_ge(pe, prev_pe)
                    a.copy(out=xt_h.ap()[:, :], in_=pslot(slot))
                    a.drain().then_inc(ac, 1)
                em("scalar", _)
                a_sp = AC.inc(1)

                def _(v, slot=slot, a_sp=a_sp):
                    v.wait_ge(ac, a_sp)
                    v.tensor_tensor(out=xt_l.ap()[:, :], in0=pslot(slot), in1=xt_h.ap()[:, :],
                                    op=OP.subtract)
                    v.drain().then_inc(dv, 1)
                em("vector", _)
                dq = DV.inc(1)
            else:
                def _(v, idx=idx, sq_l=sq_l, prev_ac=AC.n):
                    v.wait_ge(sm_x[idx % 2], sq_l)
                    v.wait_ge(ac, prev_ac)
                    v.sem_inc(dv, 1)
                em("vector", _)
                DV.inc(1)
                dq, _aq = emit_quant(pslot(slot), xt_h.ap()[:, :], xt_l.ap()[:, :])
                wprep_d1[idx] = emit_quant.last_d1
            wprep_loads[idx] = dq

        def emit_wprep_store(idx, extra_dv_wait):
            is_fw = idx >= NE
            Gi = idx - NE if is_fw else idx
            dst_h = wdt_h_d if is_fw else wot_h_d
            dst_l = wdt_l_d if is_fw else wot_l_d
            dq = wprep_loads[idx]

            def _(t, dq=dq, prev_sq=C_WP.n, xw=extra_dv_wait):
                t.wait_ge(dv, dq)
                t.wait_ge(dv, xw)
                t.wait_ge(sm_wp, prev_sq)
                ins = None
                for Ei in range(NE):
                    ins = t.transpose(out=ps_tp.ap()[:, Ei * 128:(Ei + 1) * 128],
                                      in_=xt_h.ap()[:, Ei * 128:(Ei + 1) * 128], identity=idb.ap())
                for Ei in range(NE):
                    ins = t.transpose(out=ps_tp.ap()[:, 1024 + Ei * 128:1024 + (Ei + 1) * 128],
                                      in_=xt_l.ap()[:, Ei * 128:(Ei + 1) * 128], identity=idb.ap())
                ins.then_inc(pe, 1)
            em("tensor", _)
            p_tp = PE.inc(1)

            def _(v, p_tp=p_tp):
                v.wait_ge(pe, p_tp)
                v.tensor_copy(out=AW[:, WPSP:WPSP + 1024], in_=ps_tp.ap()[:, 0:1024])
                v.drain().then_inc(dv, 1)
            em("vector", _)
            d_hi = DV.inc(1)

            def _(s, Gi=Gi, d_hi=d_hi, dst_h=dst_h):
                s.wait_ge(dv, d_hi)
                dst = bass_rust.AP(tensor=dst_h.tensor, offset=Gi * 128,
                                   ap=[[E, 128], [128 * E, NE], [1, 128]])
                s.dma_start(out=dst,
                            in_=AW[:, WPSP:WPSP + 1024].rearrange("p (e f) -> p e f", e=NE)).then_inc(sm_wp, 16)
            em("sync", _)
            sq_hi = C_WP.inc(16)

            def _(v, sq_hi=sq_hi):
                v.wait_ge(sm_wp, sq_hi)
                v.tensor_copy(out=AW[:, WPSP:WPSP + 1024], in_=ps_tp.ap()[:, 1024:2048])
                v.drain().then_inc(dv, 1)
            em("vector", _)
            d_lo = DV.inc(1)
            dv_wprep_last[idx] = d_lo

            def _(s, Gi=Gi, d_lo=d_lo, dst_l=dst_l):
                s.wait_ge(dv, d_lo)
                dst = bass_rust.AP(tensor=dst_l.tensor, offset=Gi * 128,
                                   ap=[[E, 128], [128 * E, NE], [1, 128]])
                s.dma_start(out=dst,
                            in_=AW[:, WPSP:WPSP + 1024].rearrange("p (e f) -> p e f", e=NE)).then_inc(sm_wp, 16)
            em("sync", _)
            C_WP.inc(16)

        def conv_chunks(K):
            lo = 64 * K
            out = []
            c = lo
            while c < 2048:
                hi = min((c // 512 + 1) * 512, 2048)
                out.append((c, hi))
                c = hi
            return out

        def bank_ap(c0, c1):
            if c0 < 1024:
                return ps_a.ap()[:, c0:c1]
            return ps_b.ap()[:, c0 - 1024:c1 - 1024]

        def _pads(g, pv=pe_vmm[NT - 1]):
            g.wait_ge(pe, pv)
            g.memset(AW[0:H, WB_HI:WB_HI + 128], 0.0).then_inc(gq, 1)
            g.memset(AW[0:H, WB_LO:WB_LO + 128], 0.0).then_inc(gq, 1)
        em("gpsimd", _pads)
        gq_pads = GQ.inc(2)

        gq_t = {}
        pe_convK = {}
        pe_head = [0] * H
        ac_mixh = [0] * H
        dv_mixl = [0] * H
        pe_tpg = {}
        dv_tpev = {}
        sq_spill = {}

        th_g = pslot(6).bitcast(BF16)
        tl_g = pslot(7).bitcast(BF16)

        for h in range(H):
            for Kg in range(NT // 4):
                gb = Kg % 4

                def _(g, h=h, Kg=Kg, gb=gb):
                    if h == 0 and Kg < 4:
                        g.wait_ge(dv, dv_wbufs)
                        g.wait_ge(gq, gq_pads)
                    else:
                        kk = h * (NT // 4) + Kg - 4
                        g.wait_ge(pe, pe_convK[(kk // (NT // 4), (kk % (NT // 4)) * 4 + 3)])
                    src_h = aw_ap(h * 16384 + WB_HI + 128 + 512 * Kg,
                                  [[16384, 1], [-1, 128], [128, 4], [1, 128]])
                    g.dma_start(out=th_g[:, gb * 512:(gb + 1) * 512], in_=src_h).then_inc(sm_t[gb], 16)
                    src_l = aw_ap(h * 16384 + WB_LO + 128 + 512 * Kg,
                                  [[16384, 1], [-1, 128], [128, 4], [1, 128]])
                    g.dma_start(out=tl_g[:, gb * 512:(gb + 1) * 512], in_=src_l).then_inc(sm_t[gb], 16)
                em("gpsimd", _)
                gq_t[(h, Kg)] = C_T[gb].inc(32)

            for K in range(NT):
                gb = (K // 4) % 4

                def _(t, h=h, K=K, gb=gb):
                    t.wait_ge(sm_t[gb], gq_t[(h, K // 4)])
                    if K == 0:
                        if h >= 1:
                            t.wait_ge(dv, dv_mixl[h - 1])
                        else:
                            t.wait_ge(dv, dv_vlo[NT - 1])
                    ins = None
                    ksl = gb * 512 + (K % 4) * 128
                    for pi, (stat, mvoff) in enumerate(((th_g, 0), (th_g, 32768), (tl_g, 0))):
                        for (c0, c1) in conv_chunks(K):
                            j0 = c0 - 64 * K
                            stop = (pi == 2 and c0 == 64 * K and K == 8 * (c0 // 512) + 7)
                            ins = t.matmul(out=bank_ap(c0, c1), lhsT=stat[:, ksl:ksl + 128],
                                           rhs=AV[:, mvoff + h * 2048 + j0: mvoff + h * 2048 + j0 + (c1 - c0)],
                                           start=(K == 0 and pi == 0), stop=stop)
                    ins.then_inc(pe, 1)
                em("tensor", _)
                pe_convK[(h, K)] = PE.inc(1)
            pe_head[h] = pe_convK[(h, NT - 1)]
            emit_wprep_load_dma(h)

            def _(a, h=h):
                a.wait_ge(pe, pe_head[h])
                a.copy(out=AW[:, MIXH:MIXH + 1024], in_=ps_a.ap()[:, :])
                a.copy(out=AW[:, MIXH + 1024:MIXH + 2048], in_=ps_b.ap()[:, :]).then_inc(ac, 1)
            em("scalar", _)
            ac_mixh[h] = AC.inc(1)

            def _(v, h=h):
                v.wait_ge(ac, ac_mixh[h])
                v.tensor_tensor(out=AW[:, MIXL:MIXL + 1024], in0=ps_a.ap()[:, :],
                                in1=AW[:, MIXH:MIXH + 1024], op=OP.subtract)
                v.tensor_tensor(out=AW[:, MIXL + 1024:MIXL + 2048], in0=ps_b.ap()[:, :],
                                in1=AW[:, MIXH + 1024:MIXH + 2048], op=OP.subtract).then_inc(dv, 1)
            em("vector", _)
            dv_mixl[h] = DV.inc(1)

            prow = (h % 2) * 64
            for g8 in range(8):
                half = (g8 % 2) * 1024

                def _(t, h=h, g8=g8, half=half, prow=prow):
                    t.wait_ge(dv, dv_mixl[h])
                    if (h, g8 - 2) in dv_tpev:
                        t.wait_ge(dv, dv_tpev[(h, g8 - 2)])
                    elif h >= 2:
                        t.wait_ge(dv, dv_wprep_last[h - 2])
                    elif h == 1:
                        t.wait_ge(dv, dv_tpev[(0, 6 + g8)])
                    ins = None
                    for jj in range(4):
                        J = g8 * 4 + jj
                        ins = t.transpose(out=ps_tp.ap()[prow:prow + 64, half + jj * 128: half + (jj + 1) * 128],
                                          in_=AW[:, MIXH + J * 64: MIXH + (J + 1) * 64], identity=idb.ap())
                    for jj in range(4):
                        J = g8 * 4 + jj
                        ins = t.transpose(out=ps_tp.ap()[prow:prow + 64, half + 512 + jj * 128: half + 512 + (jj + 1) * 128],
                                          in_=AW[:, MIXL + J * 64: MIXL + (J + 1) * 64], identity=idb.ap())
                    ins.then_inc(pe, 1)
                em("tensor", _)
                pe_tpg[(h, g8)] = PE.inc(1)

                def _(v, h=h, g8=g8, half=half, prow=prow):
                    v.wait_ge(pe, pe_tpg[(h, g8)])
                    if (h, g8 - 2) in sq_spill:
                        v.wait_ge(sm_sp[g8 % 2], sq_spill[(h, g8 - 2)])
                    elif h >= 2:
                        v.wait_ge(sm_sp[g8 % 2], sq_spill[(h - 2, 6 + g8)])
                    stoff = TPST[g8 % 2]
                    v.tensor_copy(out=AW[prow:prow + 64, stoff:stoff + 1024],
                                  in_=ps_tp.ap()[prow:prow + 64, half:half + 1024]).then_inc(dv, 1)
                em("vector", _)
                dv_tpev[(h, g8)] = DV.inc(1)

                def _(s, h=h, g8=g8, prow=prow):
                    s.wait_ge(dv, dv_tpev[(h, g8)])
                    stoff = TPST[g8 % 2]
                    F = h // 2
                    dst_off = g8 * 4 * 128 * 1024 + prow * 1024 + F * 128
                    dims = [[1024, 64], [128 * 1024, 4], [1, 128]]
                    s.dma_start(out=bass_rust.AP(tensor=mxt_h_d.tensor, offset=dst_off, ap=dims),
                                in_=AW[prow:prow + 64, stoff:stoff + 512]).then_inc(sm_sp[g8 % 2], 16)
                    s.dma_start(out=bass_rust.AP(tensor=mxt_l_d.tensor, offset=dst_off, ap=dims),
                                in_=AW[prow:prow + 64, stoff + 512:stoff + 1024]).then_inc(sm_sp[g8 % 2], 16)
                em("sync", _)
                sq_spill[(h, g8)] = C_SP[g8 % 2].inc(32)

            if h >= 1:
                emit_wprep_store(h - 1, dv_tpev[(h, 7)])
                if h - 1 == NE - 1:
                    wp_wo_total[0] = C_WP.n
            emit_wprep_load(h)

            if h == 9:
                def _(s, wo_total=wp_wo_total[0]):
                    s.wait_ge(pe, pe_head[NE - 1])
                    s.wait_ge(sm_wp, wo_total)
                    s.dma_start(out=AV[:, 0:8192], in_=wt_res_ap(wot_h_d)).then_inc(sm_wres, 16)
                    s.dma_start(out=AV[:, 8192:16384], in_=wt_res_ap(wot_l_d)).then_inc(sm_wres, 16)
                em("sync", _)
        emit_wprep_store(H - 1, dv_tpev[(H - 1, 7)])

        BIAS = 32768

        def _(s):
            s.wait_ge(pe, pe_head[H - 1])
            s.dma_start(out=POOL[0:1, 6144:7168], in_=bo_d.unsqueeze(0)).then_inc(sm_wres, 16)
            s.dma_start(out=POOL[0:1, 7168:8192], in_=fb_d.unsqueeze(0)).then_inc(sm_wres, 16)
        em("sync", _)
        sq_wres = 64

        def _(s, wp_total=C_WP.n):
            s.wait_ge(pe, pe_head[H - 1])
            s.wait_ge(sm_wp, wp_total)
            s.dma_start(out=AV[:, 16384:24576], in_=wt_res_ap(wdt_h_d)).then_inc(sm_wres, 16)
            s.dma_start(out=AV[:, 24576:32768], in_=wt_res_ap(wdt_l_d)).then_inc(sm_wres, 16)
        em("sync", _)
        sq_wres_all = 96

        def _(a):
            a.wait_ge(sm_wres, sq_wres)
            a.copy(out=AV[0:1, BIAS:BIAS + 1024], in_=POOL[0:1, 6144:7168])
            a.copy(out=AV[0:1, BIAS + 2048:BIAS + 3072], in_=POOL[0:1, 7168:8192])
            a.drain().then_inc(ac, 1)
        em("scalar", _)
        ac_bias = AC.inc(1)

        def _(v):
            v.wait_ge(ac, ac_bias)
            v.tensor_tensor(out=AV[0:1, BIAS + 1024:BIAS + 2048], in0=POOL[0:1, 6144:7168],
                            in1=AV[0:1, BIAS:BIAS + 1024], op=OP.subtract)
            v.tensor_tensor(out=AV[0:1, BIAS + 3072:BIAS + 4096], in0=POOL[0:1, 7168:8192],
                            in1=AV[0:1, BIAS + 2048:BIAS + 3072], op=OP.subtract)
            v.drain().then_inc(dv, 1)
        em("vector", _)
        dv_bias = DV.inc(1)

        XHP4, XLP4, XDTH4, XDTL4, M1C, M2C = 4096, 6144, 8192, 10240, 12288, 14336
        Q_ABS, Q_SC, Q_INV8, Q_SC8 = 8, 72, 136, 8
        NP = NT // 2

        def mxt_stream_ap(d, T):
            return d[T * 128:(T + 1) * 128, :]

        o2pair = POOL[:, 4096:6144]
        s0p = POOL[:, 0:2048]
        s1p = POOL[:, 2048:4096]
        m1 = AW[:, M1C:M1C + 2048]
        m2 = AW[:, M2C:M2C + 2048]
        m3 = AV[:, 61440:63488]

        XHP_SET = (AW[:, XHP4:XHP4 + 2048], AV[:, 36864:38912])
        XLP_SET = (AW[:, XLP4:XLP4 + 2048], AV[:, 38912:40960])
        XDT_SET = ((AW[:, 8192:10240], AW[:, 10240:12288]),
                   (AV[:, 59392:61440], AV[:, 63488:65536]))
        Y1R = AV[:, 40960:45056].bitcast(F32)
        Y1N = [AV[:, 45056 + 4096 * g:45056 + 4096 * (g + 1)].bitcast(F32) for g in range(3)]
        GB = AV[:, 57344:59392].bitcast(F32)

        def xdt(T):
            return XDT_SET[(T // 2) % 2][T % 2]

        sq_mt = [0] * NT
        sq_x2 = [0] * NP
        pe_o = [0] * NT
        pe_xtp2 = [0] * NT
        pe_ffn = [0] * NT
        dv_tp2 = [0] * NT
        dv_add2 = [0] * NT
        dv_add1p = [0] * NP
        ac_ln1a = [0] * NP
        ac_ln1b = [0] * NP
        ac_ln2a = [0] * NP
        ac_ln2b = [0] * NP
        dq_p = [0] * NP
        sq_out = [0] * NT

        def emit_lnA(src2, smc, wait_dv):
            def _(a, src2=src2, smc=smc, wait_dv=wait_dv):
                a.wait_ge(dv, wait_dv)
                a.activation(out=GB, in_=src2[:, 0:1024], func=AF.Identity,
                             accum_out=SM[:, smc:smc + 1])
                a.activation(out=GB, in_=src2[:, 1024:2048], func=AF.Identity,
                             accum_out=SM[:, smc + 1:smc + 2])
                a.drain()
                a.activation(out=SM[:, smc + 2:smc + 4], in_=SM[:, smc:smc + 2],
                             func=AF.Identity, scale=-INV1024)
                a.drain()
                a.activation(out=GB, in_=src2[:, 0:1024], func=AF.Square,
                             bias=SM[:, smc + 2:smc + 3], accum_out=SM[:, smc + 4:smc + 5])
                a.activation(out=GB, in_=src2[:, 1024:2048], func=AF.Square,
                             bias=SM[:, smc + 3:smc + 4], accum_out=SM[:, smc + 5:smc + 6])
                a.drain()
                a.activation(out=SM[:, smc + 6:smc + 8], in_=SM[:, smc + 4:smc + 6],
                             func=AF.Sqrt, scale=INV1024, bias=1e-5)
                a.drain().then_inc(ac, 1)
            em("scalar", _)
            return AC.inc(1)

        def emit_ln_recip(smc, wait_ac):
            def _(v, smc=smc, wait_ac=wait_ac):
                v.wait_ge(ac, wait_ac)
                v.reciprocal(out=SM[:, smc + 8:smc + 10], in_=SM[:, smc + 6:smc + 8])
                v.drain().then_inc(dv, 1)
            em("vector", _)
            return DV.inc(1)

        def emit_lnB(src2, dst0, dst1, smc, wait_dv, extra=None):
            def _(a, src2=src2, dst0=dst0, dst1=dst1, smc=smc, wait_dv=wait_dv, extra=extra):
                a.wait_ge(dv, wait_dv)
                if extra is not None:
                    extra(a)
                a.activation(out=SM[:, smc + 10:smc + 11], in_=SM[:, smc + 2:smc + 3],
                             func=AF.Identity, scale=SM[:, smc + 8:smc + 9])
                a.activation(out=SM[:, smc + 11:smc + 12], in_=SM[:, smc + 3:smc + 4],
                             func=AF.Identity, scale=SM[:, smc + 9:smc + 10])
                a.drain()
                a.activation(out=dst0, in_=src2[:, 0:1024], func=AF.Identity,
                             scale=SM[:, smc + 8:smc + 9], bias=SM[:, smc + 10:smc + 11])
                a.activation(out=dst1, in_=src2[:, 1024:2048], func=AF.Identity,
                             scale=SM[:, smc + 9:smc + 10], bias=SM[:, smc + 11:smc + 12])
                a.drain().then_inc(ac, 1)
            em("scalar", _)
            return AC.inc(1)

        def quant_gen(b):
            y1n = Y1N[b % 3]
            in3 = y1n.rearrange("p (b k) -> p b k", b=64)
            s0p3 = s0p.rearrange("p (b k) -> p b k", b=64)
            xh_dst = XHP_SET[b % 2]
            xl_dst = XLP_SET[b % 2]

            def _(v, b=b):
                v.wait_ge(ac, ac_ln1b[b])
                v.tensor_reduce(out=SM[:, Q_ABS:Q_ABS + 64], in_=in3, axis=AX.X,
                                op=OP.max, apply_absolute_value=True)
                v.drain()
                v.tensor_scalar(out=SM[:, Q_SC:Q_SC + 64], in0=SM[:, Q_ABS:Q_ABS + 64],
                                scalar1=C75I, scalar2=1e-30, op0=OP.mult, op1=OP.max)
                v.drain()
                v.reciprocal(out=SM[:, Q_INV8:Q_INV8 + 64], in_=SM[:, Q_SC:Q_SC + 64])
                v.drain()
                v.tensor_scalar(out=SM[:, Q_INV8:Q_INV8 + 64], in0=SM[:, Q_INV8:Q_INV8 + 64],
                                scalar1=8.0, scalar2=None, op0=OP.mult)
                v.tensor_scalar(out=SM[:, Q_SC8:Q_SC8 + 64], in0=SM[:, Q_SC:Q_SC + 64],
                                scalar1=0.125, scalar2=None, op0=OP.mult)
                v.drain()
                inv8bc = SM[:, Q_INV8:Q_INV8 + 64].unsqueeze(2).broadcast_to([128, 64, 32])
                v.tensor_tensor(out=s0p3, in0=in3, in1=inv8bc, op=OP.mult)
                v.drain().then_inc(dv, 1)
            em("vector", _)
            d1 = DV.inc(1)

            def _(a):
                a.wait_ge(dv, d1)
                a.activation(out=s1p, in_=s0p, func=AF.Abs)
                a.drain().then_inc(ac, 1)
            em("scalar", _)
            a1 = AC.inc(1)
            yield

            def _(v):
                v.wait_ge(ac, a1)
                v.tensor_scalar(out=m1, in0=s1p, scalar1=15.5, scalar2=0.5, op0=OP.is_le, op1=OP.max)
                v.tensor_scalar(out=m2, in0=s1p, scalar1=31.0, scalar2=0.5, op0=OP.is_le, op1=OP.max)
                v.tensor_tensor(out=m3, in0=m1, in1=m2, op=OP.mult)
                v.tensor_tensor(out=s0p, in0=s0p, in1=m3, op=OP.mult)
                v.drain().then_inc(dv, 1)
            em("vector", _)
            d2 = DV.inc(1)

            def _(a):
                a.wait_ge(dv, d2)
                a.activation(out=s1p, in_=s0p, func=AF.Copy, bias=0.5, scale=-1.0)
                a.drain().then_inc(ac, 1)
            em("scalar", _)
            a2 = AC.inc(1)
            yield

            dcv = [0] * 4
            acv = [0] * 4
            for cq in range(4):
                def _(v, cq=cq, a2=a2):
                    if cq == 0:
                        v.wait_ge(ac, a2)
                    if cq >= 2:
                        v.wait_ge(ac, acv[cq - 2])
                    ib = ibuf.ap()[:, (cq % 2) * 512:(cq % 2) * 512 + 512]
                    v.tensor_scalar(out=ib, in0=s1p[:, cq * 512:(cq + 1) * 512],
                                    scalar1=0.0, scalar2=None, op0=OP.add)
                    v.drain().then_inc(dv, 1)
                em("vector", _)
                dcv[cq] = DV.inc(1)

                def _(a, cq=cq):
                    a.wait_ge(dv, dcv[cq])
                    ib = ibuf.ap()[:, (cq % 2) * 512:(cq % 2) * 512 + 512]
                    a.copy(out=s0p[:, cq * 512:(cq + 1) * 512], in_=ib)
                    a.drain().then_inc(ac, 1)
                em("scalar", _)
                acv[cq] = AC.inc(1)
            yield

            def _(v, aend=acv[3]):
                v.wait_ge(ac, aend)
                v.tensor_scalar(out=m1, in0=m1, scalar1=-4.0, scalar2=6.0, op0=OP.mult, op1=OP.add)
                v.tensor_scalar(out=m2, in0=m2, scalar1=-1.0, scalar2=1.5, op0=OP.mult, op1=OP.add)
                v.tensor_tensor(out=m1, in0=m1, in1=m2, op=OP.mult)
                v.tensor_tensor(out=m2, in0=s0p, in1=s1p, op=OP.is_gt)
                v.tensor_tensor(out=s0p, in0=m2, in1=s0p, op=OP.subtract)
                v.tensor_tensor(out=s0p, in0=s0p, in1=m1, op=OP.mult)
                sc8bc = SM[:, Q_SC8:Q_SC8 + 64].unsqueeze(2).broadcast_to([128, 64, 32])
                v.tensor_tensor(out=s0p3, in0=s0p3, in1=sc8bc, op=OP.mult)
                v.drain().then_inc(dv, 1)
            em("vector", _)
            d4 = DV.inc(1)
            yield

            def _(a, b=b, xh_dst=xh_dst):
                a.wait_ge(dv, d4)
                if b >= 2:
                    a.wait_ge(pe, pe_xtp2[2 * (b - 2) + 1])
                a.copy(out=xh_dst, in_=s0p)
                a.drain().then_inc(ac, 1)
            em("scalar", _)
            a4 = AC.inc(1)

            def _(v, xh_dst=xh_dst, xl_dst=xl_dst):
                v.wait_ge(ac, a4)
                v.tensor_tensor(out=xl_dst, in0=s0p, in1=xh_dst, op=OP.subtract)
                v.drain().then_inc(dv, 1)
            em("vector", _)
            dq_p[b] = DV.inc(1)
            yield

        def emit_xtp2_pe(T):
            off = (T % 2) * 1024
            xh_src = XHP_SET[(T // 2) % 2]
            xl_src = XLP_SET[(T // 2) % 2]

            def _(t, T=T, off=off, xh_src=xh_src, xl_src=xl_src):
                t.wait_ge(dv, dq_p[T // 2])
                if T >= 1:
                    t.wait_ge(dv, dv_tp2[T - 1])
                ins = None
                for Ei in range(NE):
                    ins = t.transpose(out=ps_tp.ap()[:, Ei * 128:(Ei + 1) * 128],
                                      in_=xh_src[:, off + Ei * 128:off + (Ei + 1) * 128], identity=idb.ap())
                for Ei in range(NE):
                    ins = t.transpose(out=ps_tp.ap()[:, 1024 + Ei * 128:1024 + (Ei + 1) * 128],
                                      in_=xl_src[:, off + Ei * 128:off + (Ei + 1) * 128], identity=idb.ap())
                ins.then_inc(pe, 1)
            em("tensor", _)
            pe_xtp2[T] = PE.inc(1)

        def emit_xtp2_evac(T):
            def _(v, T=T):
                v.wait_ge(pe, pe_xtp2[T])
                if T >= 4:
                    v.wait_ge(pe, pe_ffn[T - 4])
                v.tensor_copy(out=xdt(T), in_=ps_tp.ap()[:, 0:2048])
                v.drain().then_inc(dv, 1)
            em("vector", _)
            dv_tp2[T] = DV.inc(1)

        def ffn_ps(T):
            return ps_b if T % 2 == 0 else ps_a

        def emit_ffn_pe(T):
            fps = ffn_ps(T)

            def _(t, T=T, fps=fps):
                if T < 2:
                    t.wait_ge(sm_wres, sq_wres_all)
                t.wait_ge(dv, dv_tp2[T])
                if T % 2 == 0:
                    if T >= 2:
                        t.wait_ge(dv, dv_add2[T - 2])
                else:
                    if T // 2 + 2 <= NP - 1:
                        t.wait_ge(dv, dv_add1p[T // 2 + 2])
                    else:
                        t.wait_ge(dv, dv_add2[T - 2])
                d = xdt(T)
                ins = None
                for Ei in range(NE):
                    xh_s = d[:, Ei * 128:(Ei + 1) * 128]
                    xl_s = d[:, 1024 + Ei * 128:1024 + (Ei + 1) * 128]
                    for pi, (stat, woff) in enumerate(((xh_s, 16384), (xh_s, 24576), (xl_s, 16384))):
                        for ch in range(2):
                            ins = t.matmul(out=fps.ap()[:, ch * 512:(ch + 1) * 512],
                                           lhsT=stat,
                                           rhs=AV[:, woff + Ei * 1024 + ch * 512: woff + Ei * 1024 + (ch + 1) * 512],
                                           start=(Ei == 0 and pi == 0), stop=False)
                for bi, boff in enumerate((BIAS + 2048, BIAS + 3072)):
                    for ch in range(2):
                        ins = t.matmul(out=fps.ap()[:, ch * 512:(ch + 1) * 512],
                                       lhsT=ones_bf.ap(),
                                       rhs=AV[0:1, boff + ch * 512: boff + (ch + 1) * 512],
                                       start=False, stop=(bi == 1))
                ins.then_inc(pe, 1)
            em("tensor", _)
            pe_ffn[T] = PE.inc(1)

        def emit_ffn_add(T):
            off = (T % 2) * 1024
            p = T // 2

            def _(v, T=T, p=p, off=off):
                v.wait_ge(pe, pe_ffn[T])
                if T >= 2:
                    v.wait_ge(sm_out, sq_out[T - 2])
                if p >= 1:
                    v.wait_ge(ac, ac_ln2b[p - 1])
                v.tensor_tensor(out=pslot(4 + (T % 2)), in0=ffn_ps(T).ap()[:, :],
                                in1=Y1N[p % 3][:, off:off + 1024], op=OP.add)
                v.drain().then_inc(dv, 1)
            em("vector", _)
            dv_add2[T] = DV.inc(1)

        def emit_opj(T, vps):
            buf = T % 2

            def _(t, T=T, buf=buf, vps=vps):
                t.wait_ge(sm_mt[buf], sq_mt[T])
                if T % 2 == 0 and T >= 6:
                    t.wait_ge(dv, dv_add2[T - 5])
                elif T >= 2:
                    t.wait_ge(dv, dv_add1p[T // 2 - 1])
                else:
                    t.wait_ge(dv, dv_mixl[H - 1])
                    t.wait_ge(dv, dv_bias)
                ins = None
                for Fi in range(NE):
                    mh = AW[:, MTH[buf] + Fi * 128: MTH[buf] + (Fi + 1) * 128]
                    ml = AW[:, MTL[buf] + Fi * 128: MTL[buf] + (Fi + 1) * 128]
                    for pi, (stat, woff) in enumerate(((mh, 0), (mh, 8192), (ml, 0))):
                        for ch in range(2):
                            ins = t.matmul(out=vps.ap()[:, ch * 512:(ch + 1) * 512],
                                           lhsT=stat,
                                           rhs=AV[:, woff + Fi * 1024 + ch * 512: woff + Fi * 1024 + (ch + 1) * 512],
                                           start=(Fi == 0 and pi == 0), stop=False)
                for bi, boff in enumerate((BIAS, BIAS + 1024)):
                    for ch in range(2):
                        ins = t.matmul(out=vps.ap()[:, ch * 512:(ch + 1) * 512],
                                       lhsT=ones_bf.ap(),
                                       rhs=AV[0:1, boff + ch * 512: boff + (ch + 1) * 512],
                                       start=False, stop=(bi == 1))
                ins.then_inc(pe, 1)
            em("tensor", _)
            pe_o[T] = PE.inc(1)

        def emit_ln2_tail(p):
            t0, t1 = 2 * p, 2 * p + 1
            dv_rc2 = emit_ln_recip(200, ac_ln2a[p])

            def slot_gate(a, p=p):
                if p >= 1:
                    a.wait_ge(sm_out, sq_out[2 * p - 1])
            ac_ln2b[p] = emit_lnB(o2pair, pslot(6), pslot(7), 200, dv_rc2, extra=slot_gate)

            def _(g, t0=t0, t1=t1, w=ac_ln2b[p]):
                g.wait_ge(ac, w)
                g.dma_start(out=y_d[t0 * 128:(t0 + 1) * 128, :], in_=pslot(6)).then_inc(sm_out, 16)
                g.dma_start(out=y_d[t1 * 128:(t1 + 1) * 128, :], in_=pslot(7)).then_inc(sm_out, 16)
            em("gpsimd", _)
            sq_out[t0] = C_OUT.inc(16)
            sq_out[t1] = C_OUT.inc(16)

        for b in range(NP):
            t0, t1 = 2 * b, 2 * b + 1
            u0, u1 = 2 * (b - 1), 2 * (b - 1) + 1
            w0, w1 = 2 * (b - 2), 2 * (b - 2) + 1

            def _(s, b=b, t0=t0, t1=t1, sp0=C_SP[0].n, sp1=C_SP[1].n):
                if b == 0:
                    s.wait_ge(sm_wres, sq_wres)
                    s.wait_ge(sm_sp[0], sp0)
                    s.wait_ge(sm_sp[1], sp1)
                    s.wait_ge(dv, dv_bias)
                if t0 >= 2:
                    s.wait_ge(pe, pe_o[t0 - 2])
                s.dma_start(out=AW[:, MTH[0]:MTH[0] + 1024],
                            in_=mxt_stream_ap(mxt_h_d, t0)).then_inc(sm_mt[0], 16)
                s.dma_start(out=AW[:, MTL[0]:MTL[0] + 1024],
                            in_=mxt_stream_ap(mxt_l_d, t0)).then_inc(sm_mt[0], 16)
                if t1 >= 2:
                    s.wait_ge(pe, pe_o[t1 - 2])
                s.dma_start(out=AW[:, MTH[1]:MTH[1] + 1024],
                            in_=mxt_stream_ap(mxt_h_d, t1)).then_inc(sm_mt[1], 16)
                s.dma_start(out=AW[:, MTL[1]:MTL[1] + 1024],
                            in_=mxt_stream_ap(mxt_l_d, t1)).then_inc(sm_mt[1], 16)
            em("sync", _)
            sq_mt[t0] = C_MT[0].inc(32)
            sq_mt[t1] = C_MT[1].inc(32)

            xs0, xs1 = (4, 5) if b == 0 else (2, 3)

            def _(s, b=b, t0=t0, t1=t1, xs0=xs0, xs1=xs1):
                if b == 0:
                    s.wait_ge(dv, wprep_d1[2 * NE - 1])
                else:
                    s.wait_ge(dv, dq_p[b - 1])
                    if b == 1:
                        s.wait_ge(dv, wprep_loads[2 * NE - 1])
                s.dma_start(out=pslot(xs0), in_=x_d[t0 * 128:(t0 + 1) * 128, :]).then_inc(sm_x[0], 16)
                s.dma_start(out=pslot(xs1), in_=x_d[t1 * 128:(t1 + 1) * 128, :]).then_inc(sm_x[1], 16)
            em("sync", _)
            sq_x2[b] = (C_X[0].inc(16), C_X[1].inc(16))

            emit_opj(t0, ps_a)
            emit_opj(t1, ps_c)

            def _(v, b=b):
                v.wait_ge(pe, pe_o[2 * b + 1])
                v.wait_ge(sm_x[0], sq_x2[b][0])
                v.wait_ge(sm_x[1], sq_x2[b][1])
                if b >= 1:
                    v.wait_ge(ac, ac_ln1b[b - 1])
                v.tensor_tensor(out=Y1R[:, 0:1024], in0=ps_a.ap()[:, :],
                                in1=pslot(4 if b == 0 else 2), op=OP.add)
                v.tensor_tensor(out=Y1R[:, 1024:2048], in0=ps_c.ap()[:, :],
                                in1=pslot(5 if b == 0 else 3), op=OP.add)
                v.drain().then_inc(dv, 1)
            em("vector", _)
            dv_add1p[b] = DV.inc(1)

            ac_ln1a[b] = emit_lnA(Y1R, 216, dv_add1p[b])

            if b >= 1:
                emit_xtp2_pe(u0)
            if b >= 2:
                emit_ffn_pe(w0)

            dv_rc1 = emit_ln_recip(216, ac_ln1a[b])

            if b >= 1:
                emit_xtp2_evac(u0)

            def y1n_gate(a, b=b):
                if b >= 3:
                    a.wait_ge(dv, dv_add2[2 * (b - 3) + 1])
            ac_ln1b[b] = emit_lnB(Y1R, Y1N[b % 3][:, 0:1024], Y1N[b % 3][:, 1024:2048],
                                  216, dv_rc1, extra=y1n_gate)

            qg = quant_gen(b)
            next(qg)
            if b >= 2:
                emit_ffn_add(w0)
            if b >= 1:
                emit_xtp2_pe(u1)
            next(qg)
            if b >= 1:
                emit_xtp2_evac(u1)
            next(qg)
            if b >= 2:
                emit_ffn_pe(w1)
                emit_ffn_add(w1)
            next(qg)
            if b >= 2:
                ac_ln2a[b - 2] = emit_lnA(o2pair, 200, dv_add2[w1])
            next(qg)
            if b >= 2:
                emit_ln2_tail(b - 2)

        emit_xtp2_pe(2 * NP - 2)
        emit_ffn_pe(2 * NP - 4)
        emit_xtp2_evac(2 * NP - 2)
        emit_ffn_add(2 * NP - 4)
        emit_xtp2_pe(2 * NP - 1)
        emit_ffn_pe(2 * NP - 3)
        emit_xtp2_evac(2 * NP - 1)
        emit_ffn_add(2 * NP - 3)
        ac_ln2a[NP - 2] = emit_lnA(o2pair, 200, dv_add2[2 * NP - 3])
        emit_ln2_tail(NP - 2)
        emit_ffn_pe(2 * NP - 2)
        emit_ffn_add(2 * NP - 2)
        emit_ffn_pe(2 * NP - 1)
        emit_ffn_add(2 * NP - 1)
        ac_ln2a[NP - 1] = emit_lnA(o2pair, 200, dv_add2[2 * NP - 1])
        emit_ln2_tail(NP - 1)

        def _(s, total=C_OUT.n):
            s.wait_ge(sm_out, total)
        em("sync", _)

        if debug_taps:
            def _(s):
                s.dma_start(out=dbg_res, in_=AV[:, 0:32768]).then_inc(sm_in, 16)
                s.dma_start(out=dbg_th[:, 0:128], in_=th_bufs[1].ap()[:, 384:512]).then_inc(sm_in, 16)
                s.dma_start(out=dbg_mh, in_=AW[:, MIXH:MIXH + 2048]).then_inc(sm_in, 16)
                s.dma_start(out=dbg_ml, in_=AW[:, MIXL:MIXL + 2048]).then_inc(sm_in, 16)
            em("sync", _)

        @block.sync
        def _(eng):
            for fn in ops["sync"]:
                fn(eng)

        @block.gpsimd
        def _(eng):
            for fn in ops["gpsimd"]:
                fn(eng)

        @block.vector
        def _(eng):
            for fn in ops["vector"]:
                fn(eng)

        @block.scalar
        def _(eng):
            for fn in ops["scalar"]:
                fn(eng)

        @block.tensor
        def _(eng):
            for fn in ops["tensor"]:
                fn(eng)

    return nc


_NC_CACHE = None


def kernel(L0_attn_x, L0_attn_W_A, L0_attn_W_V, L0_attn_W_O, L0_attn_b_O, L0_ffn_weight, L0_ffn_bias):
    global _NC_CACHE
    x = np.ascontiguousarray(np.asarray(L0_attn_x, dtype=np.float32))
    wa = np.ascontiguousarray(np.asarray(L0_attn_W_A, dtype=np.float32))
    wv = np.ascontiguousarray(np.asarray(L0_attn_W_V, dtype=np.float32))
    wo = np.ascontiguousarray(np.asarray(L0_attn_W_O, dtype=np.float32))
    bo = np.ascontiguousarray(np.asarray(L0_attn_b_O, dtype=np.float32))
    fw = np.ascontiguousarray(np.asarray(L0_ffn_weight, dtype=np.float32))
    fb = np.ascontiguousarray(np.asarray(L0_ffn_bias, dtype=np.float32))

    if _NC_CACHE is None:
        _NC_CACHE = build_nc()
    nc = _NC_CACHE

    in_maps = [dict(x=x[b], wa=wa, wv=wv, wo=wo, bo=bo, fw=fw, fb=fb) for b in range(8)]
    res = run_bass_kernel_spmd(nc, in_maps, core_ids=list(range(8)))
    out = np.stack([res.results[b]["y"] for b in range(8)], axis=0)
    return out


if __name__ == "__main__":
    rng = np.random.default_rng(0)
    ins = dict(
        L0_attn_x=rng.standard_normal((8, S, E)).astype(np.float32),
        L0_attn_W_A=(rng.standard_normal((H, E)) * E ** -0.5).astype(np.float32),
        L0_attn_W_V=(rng.standard_normal((E, E)) * E ** -0.5).astype(np.float32),
        L0_attn_W_O=(rng.standard_normal((E, E)) * E ** -0.5).astype(np.float32),
        L0_attn_b_O=(rng.standard_normal((E,)) * 0.01).astype(np.float32),
        L0_ffn_weight=(rng.standard_normal((E, E)) * 0.02).astype(np.float32),
        L0_ffn_bias=(rng.standard_normal((E,)) * 0.02).astype(np.float32),
    )
    out = kernel(**ins)
    print("out", out.shape, out.dtype, float(np.abs(out).mean()))


# version 38
# speedup vs baseline: 1.0013x; 1.0013x over previous
import numpy as np
import bass_rust
import concourse.bass as bass
import concourse.mybir as mybir
from concourse.bass_utils import run_bass_kernel_spmd

F32 = mybir.dt.float32
I32 = mybir.dt.int32
BF16 = mybir.dt.bfloat16
AF = mybir.ActivationFunctionType
OP = mybir.AluOpType
AX = mybir.AxisListType

S = 4096
E = 1024
H = 16
D = 64
NT = S // 128
NE = E // 128
C75I = float(np.float32(1.0 / 7.5))
INV1024 = float(np.float32(1.0 / 1024.0))

WB_HI = 0
WB_LO = 4224
MIXH = 8448
MIXL = 10496
XHP = 8448
XLP = 9472
TPST = (12544, 13568)
WPSP = 14592
MTH = (0, 1024)
MTL = (2048, 3072)
XH = 4096
XL = 5120
XDTH = 6144
XDTL = 7168

SM_MEAN, SM_VACC, SM_VAR, SM_SQRT, SM_INV = 0, 1, 2, 3, 4
SM_ABS, SM_SC, SM_INV8, SM_SC8 = 8, 40, 72, 104
SM_MNEG, SM_SA, SM_SB, SM_S, SM_R = 136, 137, 138, 139, 140


class Ctr:
    def __init__(self):
        self.n = 0

    def inc(self, k=1):
        self.n += k
        return self.n


def build_nc(debug_taps=False):
    nc = bass.Bass("TRN2", target_bir_lowering=False, detect_race_conditions=False)

    x_d = nc.dram_tensor("x", [S, E], F32, kind="ExternalInput").ap()
    wa_d = nc.dram_tensor("wa", [H, E], F32, kind="ExternalInput").ap()
    wv_d = nc.dram_tensor("wv", [E, E], F32, kind="ExternalInput").ap()
    wo_d = nc.dram_tensor("wo", [E, E], F32, kind="ExternalInput").ap()
    bo_d = nc.dram_tensor("bo", [E], F32, kind="ExternalInput").ap()
    fw_d = nc.dram_tensor("fw", [E, E], F32, kind="ExternalInput").ap()
    fb_d = nc.dram_tensor("fb", [E], F32, kind="ExternalInput").ap()
    y_d = nc.dram_tensor("y", [S, E], F32, kind="ExternalOutput").ap()

    if debug_taps:
        dbg_res = nc.dram_tensor("dbg_res", [128, 32768], BF16, kind="ExternalOutput").ap()
        dbg_th = nc.dram_tensor("dbg_th", [128, 128], BF16, kind="ExternalOutput").ap()
        dbg_tl = nc.dram_tensor("dbg_tl", [128, 128], BF16, kind="ExternalOutput").ap()
        dbg_mh = nc.dram_tensor("dbg_mh", [128, 2048], BF16, kind="ExternalOutput").ap()
        dbg_ml = nc.dram_tensor("dbg_ml", [128, 2048], BF16, kind="ExternalOutput").ap()
        dbg_z = nc.dram_tensor("dbg_z", [H, S], F32, kind="ExternalOutput").ap()
        dbg_wh = nc.dram_tensor("dbg_wh", [H, 4224], BF16, kind="ExternalOutput").ap()
        dbg_wl = nc.dram_tensor("dbg_wl", [H, 4224], BF16, kind="ExternalOutput").ap()
        dbg_vh = nc.dram_tensor("dbg_vh", [128, 32768], BF16, kind="ExternalOutput").ap()
        dbg_vl = nc.dram_tensor("dbg_vl", [128, 32768], BF16, kind="ExternalOutput").ap()
    wot_h_d = nc.dram_tensor("wot_h", [NE, 128, E], BF16, kind="Internal").ap()
    wot_l_d = nc.dram_tensor("wot_l", [NE, 128, E], BF16, kind="Internal").ap()
    wdt_h_d = nc.dram_tensor("wdt_h", [NE, 128, E], BF16, kind="Internal").ap()
    wdt_l_d = nc.dram_tensor("wdt_l", [NE, 128, E], BF16, kind="Internal").ap()
    mxt_h_d = nc.dram_tensor("mxt_h", [32 * 128, NE * 128], BF16, kind="Internal").ap()
    mxt_l_d = nc.dram_tensor("mxt_l", [32 * 128, NE * 128], BF16, kind="Internal").ap()

    arena_v = nc.alloc_sbuf_tensor("arena_v", [128, 65536], BF16)
    arena_w = nc.alloc_sbuf_tensor("arena_w", [128, 16384], BF16)
    pool = nc.alloc_sbuf_tensor("pool", [128, 8192], F32)
    xt_h = nc.alloc_sbuf_tensor("xt_h", [128, 1024], BF16)
    xt_l = nc.alloc_sbuf_tensor("xt_l", [128, 1024], BF16)
    ibuf = nc.alloc_sbuf_tensor("ibuf", [128, 1024], I32)
    smalls = nc.alloc_sbuf_tensor("smalls", [128, 256], F32)
    th_bufs = [nc.alloc_sbuf_tensor(f"th_{i}", [128, 512], BF16) for i in range(2)]
    tl_bufs = [nc.alloc_sbuf_tensor(f"tl_{i}", [128, 512], BF16) for i in range(2)]
    wat_h = nc.alloc_sbuf_tensor("wat_h", [128, 128], BF16)
    wat_l = nc.alloc_sbuf_tensor("wat_l", [128, 128], BF16)
    idb = nc.alloc_sbuf_tensor("idb", [128, 128], BF16)
    ones_bf = nc.alloc_sbuf_tensor("ones_bf", [1, 128], BF16)

    ps_a = nc.alloc_psum_tensor("ps_a", [128, 1024], F32)
    ps_b = nc.alloc_psum_tensor("ps_b", [128, 1024], F32)
    ps_c = nc.alloc_psum_tensor("ps_c", [128, 1024], F32)
    ps_tp = nc.alloc_psum_tensor("ps_tp", [128, 2048], BF16)

    POOL = pool.ap()
    AV = arena_v.ap()
    AW = arena_w.ap()
    SM = smalls.ap()

    def pslot(i, r0=0, r1=128):
        return POOL[r0:r1, i * 1024:(i + 1) * 1024]

    def aw_ap(offset, dims):
        return bass_rust.AP(tensor=arena_w.ap().tensor, offset=offset, ap=dims)

    def av_ap(offset, dims):
        return bass_rust.AP(tensor=arena_v.ap().tensor, offset=offset, ap=dims)

    ops = {k: [] for k in ("sync", "gpsimd", "vector", "scalar", "tensor")}

    def em(engine, fn):
        ops[engine].append(fn)

    import contextlib
    es = contextlib.ExitStack()
    with es:
        pe = es.enter_context(nc.semaphore("PE"))
        dv = es.enter_context(nc.semaphore("DV"))
        ac = es.enter_context(nc.semaphore("AC"))
        gq = es.enter_context(nc.semaphore("GQ"))
        sm_in = es.enter_context(nc.semaphore("sm_in"))
        sm_wv0 = es.enter_context(nc.semaphore("sm_wv0"))
        sm_wv1 = es.enter_context(nc.semaphore("sm_wv1"))
        sm_x0 = es.enter_context(nc.semaphore("sm_x0"))
        sm_x1 = es.enter_context(nc.semaphore("sm_x1"))
        sm_wp = es.enter_context(nc.semaphore("sm_wp"))
        sm_sp0 = es.enter_context(nc.semaphore("sm_sp0"))
        sm_sp1 = es.enter_context(nc.semaphore("sm_sp1"))
        sm_mt0 = es.enter_context(nc.semaphore("sm_mt0"))
        sm_mt1 = es.enter_context(nc.semaphore("sm_mt1"))
        sm_wres = es.enter_context(nc.semaphore("sm_wres"))
        sm_t0 = es.enter_context(nc.semaphore("sm_t0"))
        sm_t1 = es.enter_context(nc.semaphore("sm_t1"))
        sm_t2 = es.enter_context(nc.semaphore("sm_t2"))
        sm_t3 = es.enter_context(nc.semaphore("sm_t3"))
        sm_out = es.enter_context(nc.semaphore("sm_out"))
        block = es.enter_context(nc.Block())

        sm_wv = (sm_wv0, sm_wv1)
        sm_x = (sm_x0, sm_x1)
        sm_sp = (sm_sp0, sm_sp1)
        sm_mt = (sm_mt0, sm_mt1)
        sm_t = (sm_t0, sm_t1, sm_t2, sm_t3)
        PE, DV, AC, GQ = Ctr(), Ctr(), Ctr(), Ctr()
        C_WV = [Ctr(), Ctr()]
        C_X = [Ctr(), Ctr()]
        C_WP = Ctr()
        C_SP = [Ctr(), Ctr()]
        C_MT = [Ctr(), Ctr()]
        C_T = [Ctr(), Ctr(), Ctr(), Ctr()]
        C_OUT = Ctr()

        def emit_quant(in_ap, out_xh_ap, out_xl_ap, pre_sq_wait=None):
            s0, s1, s2, s3 = pslot(0), pslot(1), pslot(2), pslot(3)
            in3 = in_ap.rearrange("p (b k) -> p b k", b=32)
            s0_3 = s0.rearrange("p (b k) -> p b k", b=32)

            def _(v):
                if pre_sq_wait is not None:
                    v.wait_ge(sm_out, pre_sq_wait)
                v.tensor_reduce(out=SM[:, SM_ABS:SM_ABS + 32], in_=in3, axis=AX.X,
                                op=OP.max, apply_absolute_value=True)
                v.drain()
                v.tensor_scalar(out=SM[:, SM_SC:SM_SC + 32], in0=SM[:, SM_ABS:SM_ABS + 32],
                                scalar1=C75I, scalar2=1e-30, op0=OP.mult, op1=OP.max)
                v.drain()
                v.reciprocal(out=SM[:, SM_INV8:SM_INV8 + 32], in_=SM[:, SM_SC:SM_SC + 32])
                v.drain()
                v.tensor_scalar(out=SM[:, SM_INV8:SM_INV8 + 32], in0=SM[:, SM_INV8:SM_INV8 + 32],
                                scalar1=8.0, scalar2=None, op0=OP.mult)
                v.tensor_scalar(out=SM[:, SM_SC8:SM_SC8 + 32], in0=SM[:, SM_SC:SM_SC + 32],
                                scalar1=0.125, scalar2=None, op0=OP.mult)
                v.drain()
                inv8bc = SM[:, SM_INV8:SM_INV8 + 32].unsqueeze(2).broadcast_to([128, 32, 32])
                v.tensor_tensor(out=s0_3, in0=in3, in1=inv8bc, op=OP.mult)
                v.drain().then_inc(dv, 1)
            em("vector", _)
            d1 = DV.inc(1)
            emit_quant.last_d1 = d1

            def _(a):
                a.wait_ge(dv, d1)
                a.activation(out=s1, in_=s0, func=AF.Abs)
                a.drain().then_inc(ac, 1)
            em("scalar", _)
            a1 = AC.inc(1)

            def _(v):
                v.wait_ge(ac, a1)
                v.tensor_scalar(out=s2, in0=s1, scalar1=15.5, scalar2=0.5, op0=OP.is_le, op1=OP.max)
                v.tensor_scalar(out=s3, in0=s1, scalar1=31.0, scalar2=0.5, op0=OP.is_le, op1=OP.max)
                v.tensor_tensor(out=s1, in0=s0, in1=s2, op=OP.mult)
                v.tensor_tensor(out=s0, in0=s1, in1=s3, op=OP.mult)
                v.drain().then_inc(dv, 1)
            em("vector", _)
            d2 = DV.inc(1)

            def _(a):
                a.wait_ge(dv, d2)
                a.activation(out=s1, in_=s0, func=AF.Copy, bias=0.5, scale=-1.0)
                a.drain().then_inc(ac, 1)
            em("scalar", _)
            a2 = AC.inc(1)

            def _(v):
                v.wait_ge(ac, a2)
                v.tensor_scalar(out=ibuf.ap()[:, 0:512], in0=s1[:, 0:512],
                                scalar1=0.0, scalar2=None, op0=OP.add)
                v.drain().then_inc(dv, 1)
            em("vector", _)
            d3a = DV.inc(1)

            def _(a):
                a.wait_ge(dv, d3a)
                a.copy(out=s0[:, 0:512], in_=ibuf.ap()[:, 0:512])
                a.drain().then_inc(ac, 1)
            em("scalar", _)
            a3a = AC.inc(1)

            def _(v):
                v.wait_ge(ac, a3a)
                v.tensor_scalar(out=ibuf.ap()[:, 0:512], in0=s1[:, 512:1024],
                                scalar1=0.0, scalar2=None, op0=OP.add)
                v.drain().then_inc(dv, 1)
            em("vector", _)
            d3b = DV.inc(1)

            def _(a):
                a.wait_ge(dv, d3b)
                a.copy(out=s0[:, 512:1024], in_=ibuf.ap()[:, 0:512])
                a.drain().then_inc(ac, 1)
            em("scalar", _)
            a3b = AC.inc(1)

            def _(v):
                v.wait_ge(ac, a3b)
                v.tensor_scalar(out=s2, in0=s2, scalar1=-4.0, scalar2=6.0, op0=OP.mult, op1=OP.add)
                v.tensor_scalar(out=s3, in0=s3, scalar1=-1.0, scalar2=1.5, op0=OP.mult, op1=OP.add)
                v.tensor_tensor(out=s2, in0=s2, in1=s3, op=OP.mult)
                v.tensor_tensor(out=s3, in0=s0, in1=s1, op=OP.is_gt)
                v.tensor_tensor(out=s0, in0=s3, in1=s0, op=OP.subtract)
                v.tensor_tensor(out=s0, in0=s0, in1=s2, op=OP.mult)
                sc8bc = SM[:, SM_SC8:SM_SC8 + 32].unsqueeze(2).broadcast_to([128, 32, 32])
                v.tensor_tensor(out=s0_3, in0=s0_3, in1=sc8bc, op=OP.mult)
                v.drain().then_inc(dv, 1)
            em("vector", _)
            d4 = DV.inc(1)

            def _(a):
                a.wait_ge(dv, d4)
                a.copy(out=out_xh_ap, in_=s0)
                a.drain().then_inc(ac, 1)
            em("scalar", _)
            a4 = AC.inc(1)

            def _(v):
                v.wait_ge(ac, a4)
                v.tensor_tensor(out=out_xl_ap, in0=s0, in1=out_xh_ap, op=OP.subtract)
                v.drain().then_inc(dv, 1)
            em("vector", _)
            d5 = DV.inc(1)
            return d5, a4

        def _(s):
            s.dma_start(out=POOL[0:H, 0:1024], in_=wa_d[:, :]).then_inc(sm_in, 16)
        em("sync", _)
        sq_wa = 16

        def _(g):
            g.memset(idb.ap(), 0.0).then_inc(gq, 1)
            g.memset(ones_bf.ap(), 1.0).then_inc(gq, 1)
            g.memset(idb.ap()[0:1, 0:1], 0.0).then_inc(gq, 1)
            g.memset(idb.ap()[0:1, 0:1], 0.0).then_inc(gq, 1)
            g.wait_ge(gq, 4)
            g.affine_select(out=idb.ap(), in_=idb.ap(), compare_op=OP.not_equal,
                            fill=1.0, base=0, pattern=[[-1, 128]], channel_multiplier=1).then_inc(gq, 1)
            g.memset(SM[:, 240:241], 1e-5).then_inc(gq, 1)
            g.memset(SM[:, 241:242], INV1024).then_inc(gq, 1)
            g.memset(SM[:, 242:243], -INV1024).then_inc(gq, 1)
        em("gpsimd", _)
        gq_setup = GQ.inc(8)
        nc.const_aps.register(F32, 1e-5, SM[:, 240:241])
        nc.const_aps.register(F32, INV1024, SM[:, 241:242])
        nc.const_aps.register(F32, -INV1024, SM[:, 242:243])

        def _(v):
            v.wait_ge(sm_in, sq_wa)
            v.tensor_scalar(out=POOL[0:H, 3072:4096], in0=POOL[0:H, 0:1024],
                            scalar1=0.125, scalar2=None, op0=OP.mult)
            v.drain().then_inc(dv, 1)
        em("vector", _)
        dv_was = DV.inc(1)

        def _(a):
            a.wait_ge(dv, dv_was)
            a.copy(out=xt_h.ap()[0:H, 0:1024], in_=POOL[0:H, 3072:4096]).then_inc(ac, 1)
        em("scalar", _)
        ac_w0 = AC.inc(1)

        def _(v):
            v.wait_ge(ac, ac_w0)
            v.tensor_tensor(out=xt_l.ap()[0:H, 0:1024], in0=POOL[0:H, 3072:4096],
                            in1=xt_h.ap()[0:H, 0:1024], op=OP.subtract).then_inc(dv, 1)
        em("vector", _)
        dv_w0 = DV.inc(1)

        def _(t):
            t.wait_ge(gq, gq_setup)
            t.wait_ge(dv, dv_w0)
            ins = None
            for Ei in range(NE):
                ins = t.transpose(out=ps_tp.ap()[:, Ei * 16:(Ei + 1) * 16],
                                  in_=xt_h.ap()[0:H, Ei * 128:(Ei + 1) * 128],
                                  identity=idb.ap()[0:H, 0:H])
            for Ei in range(NE):
                ins = t.transpose(out=ps_tp.ap()[:, 128 + Ei * 16:128 + (Ei + 1) * 16],
                                  in_=xt_l.ap()[0:H, Ei * 128:(Ei + 1) * 128],
                                  identity=idb.ap()[0:H, 0:H])
            ins.then_inc(pe, 1)
        em("tensor", _)
        pe_wat = PE.inc(1)

        def _(v):
            v.wait_ge(pe, pe_wat)
            v.tensor_copy(out=wat_h.ap()[:, 0:128], in_=ps_tp.ap()[:, 0:128])
            v.tensor_copy(out=wat_l.ap()[:, 0:128], in_=ps_tp.ap()[:, 128:256])
            v.drain().then_inc(dv, 1)
        em("vector", _)
        dv_wat = DV.inc(1)

        sq_wv = [0] * NE
        ac_wvs = [0] * NE
        dv_wvl = [0] * NE
        pe_wvt = [0] * NE
        dv_wve = [0] * NE
        wvbh = (xt_h.ap()[:, :], pslot(6).bitcast(BF16)[:, 0:1024])
        wvbl = (xt_l.ap()[:, :], pslot(7).bitcast(BF16)[:, 0:1024])

        def emit_wve(Fi):
            def _(v, Fi=Fi):
                v.wait_ge(pe, pe_wvt[Fi])
                dst_h = aw_ap(Fi * 128, [[16384, 128], [1024, NE], [1, 128]])
                v.tensor_copy(out=dst_h, in_=ps_tp.ap()[:, 0:1024].rearrange("p (e f) -> p e f", e=NE))
                dst_l = aw_ap(8192 + Fi * 128, [[16384, 128], [1024, NE], [1, 128]])
                v.tensor_copy(out=dst_l, in_=ps_tp.ap()[:, 1024:2048].rearrange("p (e f) -> p e f", e=NE)).then_inc(dv, 1)
            em("vector", _)
            dv_wve[Fi] = DV.inc(1)

        for Fi in range(NE):
            slot = 4 + (Fi % 2)
            WH = wvbh[Fi % 2]
            WL = wvbl[Fi % 2]

            def _(s, Fi=Fi, slot=slot):
                if Fi >= 2:
                    s.wait_ge(dv, dv_wvl[Fi - 2])
                s.dma_start(out=pslot(slot), in_=wv_d[Fi * 128:(Fi + 1) * 128, :]).then_inc(sm_wv[Fi % 2], 16)
            em("sync", _)
            sq_wv[Fi] = C_WV[Fi % 2].inc(16)

            def _(a, Fi=Fi, slot=slot, WH=WH):
                a.wait_ge(sm_wv[Fi % 2], sq_wv[Fi])
                if Fi >= 2:
                    a.wait_ge(pe, pe_wvt[Fi - 2])
                elif Fi == 0:
                    a.wait_ge(pe, pe_wat)
                a.copy(out=WH, in_=pslot(slot)).then_inc(ac, 1)
            em("scalar", _)
            ac_wvs[Fi] = AC.inc(1)

            def _(v, Fi=Fi, slot=slot, WH=WH, WL=WL):
                v.wait_ge(ac, ac_wvs[Fi])
                v.tensor_tensor(out=WL, in0=pslot(slot), in1=WH, op=OP.subtract).then_inc(dv, 1)
            em("vector", _)
            dv_wvl[Fi] = DV.inc(1)

            if Fi >= 1:
                emit_wve(Fi - 1)

            def _(t, Fi=Fi, WH=WH, WL=WL):
                t.wait_ge(dv, dv_wvl[Fi])
                if Fi >= 1:
                    t.wait_ge(dv, dv_wve[Fi - 1])
                ins = None
                for Ei in range(NE):
                    ins = t.transpose(out=ps_tp.ap()[:, Ei * 128:(Ei + 1) * 128],
                                      in_=WH[:, Ei * 128:(Ei + 1) * 128], identity=idb.ap())
                for Ei in range(NE):
                    ins = t.transpose(out=ps_tp.ap()[:, 1024 + Ei * 128:1024 + (Ei + 1) * 128],
                                      in_=WL[:, Ei * 128:(Ei + 1) * 128], identity=idb.ap())
                ins.then_inc(pe, 1)
            em("tensor", _)
            pe_wvt[Fi] = PE.inc(1)
        emit_wve(NE - 1)

        sq_x = [0] * NT
        ac_xhp = [0] * NT
        ac_xte = [0] * NT
        dv_xlp = [0] * NT
        pe_xtp = [0] * NT
        dv_xte = [0] * NT
        pe_zmm = [0] * NT
        pe_vmm = [0] * NT
        dv_zev = [0] * NT
        ac_vhi = [0] * NT
        dv_vlo = [0] * NT

        xbh = (xt_h.ap()[:, :], pslot(6).bitcast(BF16)[:, 0:1024])
        xbl = (xt_l.ap()[:, :], pslot(7).bitcast(BF16)[:, 0:1024])

        def emit_vevac(T):
            vps = ps_b if (T % 2 == 0) else ps_c

            def _(a, T=T, vps=vps):
                a.wait_ge(pe, pe_vmm[T])
                dst = av_ap(T * 64, [[65536, 128], [2048, H], [1, 64]])
                a.copy(out=dst, in_=vps.ap().rearrange("p (h d) -> p h d", h=H)).then_inc(ac, 1)
            em("scalar", _)
            ac_vhi[T] = AC.inc(1)

            def _(v, T=T, vps=vps):
                v.wait_ge(ac, ac_vhi[T])
                hi = av_ap(T * 64, [[65536, 128], [2048, H], [1, 64]])
                lo = av_ap(32768 + T * 64, [[65536, 128], [2048, H], [1, 64]])
                v.tensor_tensor(out=lo, in0=vps.ap().rearrange("p (h d) -> p h d", h=H),
                                in1=hi, op=OP.subtract).then_inc(dv, 1)
            em("vector", _)
            dv_vlo[T] = DV.inc(1)

        for T in range(NT):
            slot = 4 + (T % 2)
            XH = xbh[T % 2]
            XL = xbl[T % 2]

            def _(s, T=T, slot=slot):
                if T >= 2:
                    s.wait_ge(dv, dv_xlp[T - 2])
                else:
                    s.wait_ge(dv, dv_wvl[NE - 1])
                s.dma_start(out=pslot(slot), in_=x_d[T * 128:(T + 1) * 128, :]).then_inc(sm_x[T % 2], 16)
            em("sync", _)
            sq_x[T] = C_X[T % 2].inc(16)

            def _(a, T=T, slot=slot, XH=XH):
                a.wait_ge(sm_x[T % 2], sq_x[T])
                if T >= 2:
                    a.wait_ge(pe, pe_vmm[T - 2])
                else:
                    a.wait_ge(pe, pe_wvt[NE - 1])
                a.copy(out=XH, in_=pslot(slot)).then_inc(ac, 1)
            em("scalar", _)
            ac_xhp[T] = AC.inc(1)

            def _(v, T=T, slot=slot, XH=XH, XL=XL):
                v.wait_ge(ac, ac_xhp[T])
                v.tensor_tensor(out=XL, in0=pslot(slot), in1=XH, op=OP.subtract).then_inc(dv, 1)
            em("vector", _)
            dv_xlp[T] = DV.inc(1)

            if T >= 1:
                emit_vevac(T - 1)

            def _(t, T=T, XH=XH, XL=XL):
                t.wait_ge(dv, dv_xlp[T])
                if T == 0:
                    t.wait_ge(dv, dv_wve[NE - 1])
                else:
                    t.wait_ge(dv, dv_xte[T - 1])
                    t.wait_ge(ac, ac_xte[T - 1])
                ins = None
                for Ei in range(NE):
                    ins = t.transpose(out=ps_tp.ap()[:, Ei * 128:(Ei + 1) * 128],
                                      in_=XH[:, Ei * 128:(Ei + 1) * 128], identity=idb.ap())
                for Ei in range(NE):
                    ins = t.transpose(out=ps_tp.ap()[:, 1024 + Ei * 128:1024 + (Ei + 1) * 128],
                                      in_=XL[:, Ei * 128:(Ei + 1) * 128], identity=idb.ap())
                ins.then_inc(pe, 1)
            em("tensor", _)
            pe_xtp[T] = PE.inc(1)

            def _(a, T=T, XH=XH):
                a.wait_ge(pe, pe_xtp[T])
                a.copy(out=XH, in_=ps_tp.ap()[:, 0:1024]).then_inc(ac, 1)
            em("scalar", _)
            ac_xte[T] = AC.inc(1)

            def _(v, T=T, XL=XL):
                v.wait_ge(pe, pe_xtp[T])
                v.tensor_copy(out=XL, in_=ps_tp.ap()[:, 1024:2048]).then_inc(dv, 1)
            em("vector", _)
            dv_xte[T] = DV.inc(1)

            def _(t, T=T, XH=XH, XL=XL):
                t.wait_ge(dv, dv_xte[T])
                t.wait_ge(ac, ac_xte[T])
                if T >= 1:
                    t.wait_ge(dv, dv_zev[T - 1])
                if T == 0:
                    t.wait_ge(dv, dv_wat)
                ins = None
                first = True
                for Ei in range(NE):
                    for wt, xs in ((wat_h, XH), (wat_h, XL), (wat_l, XH)):
                        ins = t.matmul(out=ps_a.ap()[0:H, 0:128],
                                       lhsT=wt.ap()[:, Ei * 16:(Ei + 1) * 16],
                                       rhs=xs[:, Ei * 128:(Ei + 1) * 128],
                                       start=first, stop=(Ei == NE - 1 and wt is wat_l and xs is XH))
                        first = False
                ins.then_inc(pe, 1)
            em("tensor", _)
            pe_zmm[T] = PE.inc(1)

            vps = ps_b if (T % 2 == 0) else ps_c

            def _(t, T=T, vps=vps, XH=XH, XL=XL):
                if T >= 2:
                    t.wait_ge(dv, dv_vlo[T - 2])
                ins = None
                for Ei in range(NE):
                    for pi, (xs, woff) in enumerate(((XH, 0), (XH, 8192), (XL, 0))):
                        for ch in range(2):
                            ins = t.matmul(
                                out=vps.ap()[:, ch * 512:(ch + 1) * 512],
                                lhsT=xs[:, Ei * 128:(Ei + 1) * 128],
                                rhs=AW[:, woff + Ei * 1024 + ch * 512: woff + Ei * 1024 + (ch + 1) * 512],
                                start=(Ei == 0 and pi == 0), stop=(Ei == NE - 1 and pi == 2))
                ins.then_inc(pe, 1)
            em("tensor", _)
            pe_vmm[T] = PE.inc(1)

            def _(v, T=T):
                v.wait_ge(pe, pe_zmm[T])
                v.tensor_copy(out=POOL[0:H, T * 128:(T + 1) * 128], in_=ps_a.ap()[0:H, 0:128]).then_inc(dv, 1)
            em("vector", _)
            dv_zev[T] = DV.inc(1)

        emit_vevac(NT - 1)

        def _(v):
            v.drain()
            v.reduce_max(out=SM[0:H, SM_MNEG:SM_MNEG + 1], in_=POOL[0:H, 0:4096],
                         axis=AX.X, negate=True)
            v.drain().then_inc(dv, 1)
        em("vector", _)
        dv_mneg = DV.inc(1)

        def _(a):
            a.wait_ge(dv, dv_mneg)
            a.wait_ge(pe, pe_vmm[NT - 1])
            a.activation(out=POOL[0:H, 6144:8192], in_=POOL[0:H, 0:2048], func=AF.Exp,
                         bias=SM[0:H, SM_MNEG:SM_MNEG + 1], scale=1.0).then_inc(ac, 1)
        em("scalar", _)
        ac_expA = AC.inc(1)

        def _(v):
            v.wait_ge(ac, ac_expA)
            v.reduce_sum(out=SM[0:H, SM_SA:SM_SA + 1], in_=POOL[0:H, 6144:8192], axis=AX.X)
            v.drain().then_inc(dv, 1)
        em("vector", _)
        dv_sA = DV.inc(1)

        def _(a):
            a.wait_ge(dv, dv_sA)
            a.activation(out=POOL[0:H, 6144:8192], in_=POOL[0:H, 2048:4096], func=AF.Exp,
                         bias=SM[0:H, SM_MNEG:SM_MNEG + 1], scale=1.0).then_inc(ac, 1)
        em("scalar", _)
        ac_exp = AC.inc(1)

        def _(v):
            v.wait_ge(ac, ac_exp)
            v.reduce_sum(out=SM[0:H, SM_SB:SM_SB + 1], in_=POOL[0:H, 6144:8192], axis=AX.X)
            v.drain()
            v.tensor_tensor(out=SM[0:H, SM_S:SM_S + 1], in0=SM[0:H, SM_SA:SM_SA + 1],
                            in1=SM[0:H, SM_SB:SM_SB + 1], op=OP.add)
            v.drain()
            v.reciprocal(out=SM[0:H, SM_R:SM_R + 1], in_=SM[0:H, SM_S:SM_S + 1])
            v.drain().then_inc(dv, 1)
        em("vector", _)
        dv_r = DV.inc(1)

        def _(a):
            a.wait_ge(dv, dv_r)
            a.wait_ge(gq, gq_setup)
            a.activation(out=AW[0:H, WB_HI + 128 + 2048: WB_HI + 128 + 4096],
                         in_=POOL[0:H, 6144:8192], func=AF.Copy,
                         scale=SM[0:H, SM_R:SM_R + 1]).then_inc(ac, 1)
        em("scalar", _)
        ac_whB = AC.inc(1)

        def _(v):
            v.wait_ge(ac, ac_whB)
            v.scalar_tensor_tensor(out=AW[0:H, WB_LO + 128 + 2048: WB_LO + 128 + 4096],
                                   in0=POOL[0:H, 6144:8192], scalar=SM[0:H, SM_R:SM_R + 1],
                                   in1=AW[0:H, WB_HI + 128 + 2048: WB_HI + 128 + 4096],
                                   op0=OP.mult, op1=OP.subtract)
            v.drain().then_inc(dv, 1)
        em("vector", _)
        dv_wlB = DV.inc(1)

        def _(a):
            a.wait_ge(dv, dv_wlB)
            a.activation(out=POOL[0:H, 6144:8192], in_=POOL[0:H, 0:2048], func=AF.Exp,
                         bias=SM[0:H, SM_MNEG:SM_MNEG + 1], scale=1.0)
            a.activation(out=AW[0:H, WB_HI + 128: WB_HI + 128 + 2048],
                         in_=POOL[0:H, 6144:8192], func=AF.Copy,
                         scale=SM[0:H, SM_R:SM_R + 1]).then_inc(ac, 1)
        em("scalar", _)
        ac_whA = AC.inc(1)

        def _(v):
            v.wait_ge(ac, ac_whA)
            v.scalar_tensor_tensor(out=AW[0:H, WB_LO + 128: WB_LO + 128 + 2048],
                                   in0=POOL[0:H, 6144:8192], scalar=SM[0:H, SM_R:SM_R + 1],
                                   in1=AW[0:H, WB_HI + 128: WB_HI + 128 + 2048],
                                   op0=OP.mult, op1=OP.subtract)
            v.drain().then_inc(dv, 1)
        em("vector", _)
        dv_wbufs = DV.inc(1)

        if debug_taps:
            def _(s):
                s.wait_ge(dv, dv_mneg)
                s.dma_start(out=dbg_z[:, :], in_=POOL[0:H, 0:4096]).then_inc(sm_in, 16)
                s.wait_ge(dv, dv_wbufs)
                s.dma_start(out=dbg_wh[:, :], in_=AW[0:H, WB_HI:WB_HI + 4224]).then_inc(sm_in, 16)
                s.dma_start(out=dbg_wl[:, :], in_=AW[0:H, WB_LO:WB_LO + 4224]).then_inc(sm_in, 16)
                s.dma_start(out=dbg_vh[:, :], in_=AV[:, 0:32768]).then_inc(sm_in, 16)
                s.dma_start(out=dbg_vl[:, :], in_=AV[:, 32768:65536]).then_inc(sm_in, 16)
            em("sync", _)

        def wt_res_ap(d):
            return bass_rust.AP(tensor=d.tensor, offset=0,
                                ap=[[E, 128], [128 * E, NE], [1, E]])

        wp_wo_total = [0]
        wprep_d1 = {}
        dv_wprep_last = [0] * (2 * NE)
        wprep_loads = {}

        wprep_sql = {}

        def emit_wprep_load_dma(idx):
            is_fw = idx >= NE
            Gi = idx - NE if is_fw else idx
            slot = 4 + (idx % 2)
            w_src = fw_d if is_fw else wo_d

            def _(s, idx=idx, Gi=Gi, slot=slot, w_src=w_src):
                if idx < 2:
                    s.wait_ge(dv, dv_xlp[NT - 1])
                else:
                    s.wait_ge(dv, dv_wprep_last[idx - 2])
                s.dma_start(out=pslot(slot), in_=w_src[Gi * 128:(Gi + 1) * 128, :]).then_inc(sm_x[idx % 2], 16)
            em("sync", _)
            wprep_sql[idx] = C_X[idx % 2].inc(16)

        def emit_wprep_load(idx):
            is_fw = idx >= NE
            Gi = idx - NE if is_fw else idx
            slot = 4 + (idx % 2)
            sq_l = wprep_sql[idx]

            if not is_fw:
                def _(a, slot=slot, sq_l=sq_l, idx=idx, prev_pe=PE.n):
                    a.wait_ge(sm_x[idx % 2], sq_l)
                    a.wait_ge(pe, prev_pe)
                    a.copy(out=xt_h.ap()[:, :], in_=pslot(slot))
                    a.drain().then_inc(ac, 1)
                em("scalar", _)
                a_sp = AC.inc(1)

                def _(v, slot=slot, a_sp=a_sp):
                    v.wait_ge(ac, a_sp)
                    v.tensor_tensor(out=xt_l.ap()[:, :], in0=pslot(slot), in1=xt_h.ap()[:, :],
                                    op=OP.subtract)
                    v.drain().then_inc(dv, 1)
                em("vector", _)
                dq = DV.inc(1)
            else:
                def _(v, idx=idx, sq_l=sq_l, prev_ac=AC.n):
                    v.wait_ge(sm_x[idx % 2], sq_l)
                    v.wait_ge(ac, prev_ac)
                    v.sem_inc(dv, 1)
                em("vector", _)
                DV.inc(1)
                dq, _aq = emit_quant(pslot(slot), xt_h.ap()[:, :], xt_l.ap()[:, :])
                wprep_d1[idx] = emit_quant.last_d1
            wprep_loads[idx] = dq

        def emit_wprep_store(idx, extra_dv_wait):
            is_fw = idx >= NE
            Gi = idx - NE if is_fw else idx
            dst_h = wdt_h_d if is_fw else wot_h_d
            dst_l = wdt_l_d if is_fw else wot_l_d
            dq = wprep_loads[idx]

            def _(t, dq=dq, prev_sq=C_WP.n, xw=extra_dv_wait):
                t.wait_ge(dv, dq)
                t.wait_ge(dv, xw)
                t.wait_ge(sm_wp, prev_sq)
                ins = None
                for Ei in range(NE):
                    ins = t.transpose(out=ps_tp.ap()[:, Ei * 128:(Ei + 1) * 128],
                                      in_=xt_h.ap()[:, Ei * 128:(Ei + 1) * 128], identity=idb.ap())
                for Ei in range(NE):
                    ins = t.transpose(out=ps_tp.ap()[:, 1024 + Ei * 128:1024 + (Ei + 1) * 128],
                                      in_=xt_l.ap()[:, Ei * 128:(Ei + 1) * 128], identity=idb.ap())
                ins.then_inc(pe, 1)
            em("tensor", _)
            p_tp = PE.inc(1)

            def _(v, p_tp=p_tp):
                v.wait_ge(pe, p_tp)
                v.tensor_copy(out=AW[:, WPSP:WPSP + 1024], in_=ps_tp.ap()[:, 0:1024])
                v.drain().then_inc(dv, 1)
            em("vector", _)
            d_hi = DV.inc(1)

            def _(s, Gi=Gi, d_hi=d_hi, dst_h=dst_h):
                s.wait_ge(dv, d_hi)
                dst = bass_rust.AP(tensor=dst_h.tensor, offset=Gi * 128,
                                   ap=[[E, 128], [128 * E, NE], [1, 128]])
                s.dma_start(out=dst,
                            in_=AW[:, WPSP:WPSP + 1024].rearrange("p (e f) -> p e f", e=NE)).then_inc(sm_wp, 16)
            em("sync", _)
            sq_hi = C_WP.inc(16)

            def _(v, sq_hi=sq_hi):
                v.wait_ge(sm_wp, sq_hi)
                v.tensor_copy(out=AW[:, WPSP:WPSP + 1024], in_=ps_tp.ap()[:, 1024:2048])
                v.drain().then_inc(dv, 1)
            em("vector", _)
            d_lo = DV.inc(1)
            dv_wprep_last[idx] = d_lo

            def _(s, Gi=Gi, d_lo=d_lo, dst_l=dst_l):
                s.wait_ge(dv, d_lo)
                dst = bass_rust.AP(tensor=dst_l.tensor, offset=Gi * 128,
                                   ap=[[E, 128], [128 * E, NE], [1, 128]])
                s.dma_start(out=dst,
                            in_=AW[:, WPSP:WPSP + 1024].rearrange("p (e f) -> p e f", e=NE)).then_inc(sm_wp, 16)
            em("sync", _)
            C_WP.inc(16)

        def conv_chunks(K):
            lo = 64 * K
            out = []
            c = lo
            while c < 2048:
                hi = min((c // 512 + 1) * 512, 2048)
                out.append((c, hi))
                c = hi
            return out

        def bank_ap(c0, c1):
            if c0 < 1024:
                return ps_a.ap()[:, c0:c1]
            return ps_b.ap()[:, c0 - 1024:c1 - 1024]

        def _pads(g, pv=pe_vmm[NT - 1]):
            g.wait_ge(pe, pv)
            g.memset(AW[0:H, WB_HI:WB_HI + 128], 0.0).then_inc(gq, 1)
            g.memset(AW[0:H, WB_LO:WB_LO + 128], 0.0).then_inc(gq, 1)
        em("gpsimd", _pads)
        gq_pads = GQ.inc(2)

        gq_t = {}
        pe_convK = {}
        pe_head = [0] * H
        ac_mixh = [0] * H
        dv_mixl = [0] * H
        pe_tpg = {}
        dv_tpev = {}
        sq_spill = {}

        th_g = pslot(6).bitcast(BF16)
        tl_g = pslot(7).bitcast(BF16)

        for h in range(H):
            for Kg in range(NT // 4):
                gb = Kg % 4

                def _(g, h=h, Kg=Kg, gb=gb):
                    if h == 0 and Kg < 4:
                        g.wait_ge(dv, dv_wbufs)
                        g.wait_ge(gq, gq_pads)
                    else:
                        kk = h * (NT // 4) + Kg - 4
                        g.wait_ge(pe, pe_convK[(kk // (NT // 4), (kk % (NT // 4)) * 4 + 3)])
                    src_h = aw_ap(h * 16384 + WB_HI + 128 + 512 * Kg,
                                  [[16384, 1], [-1, 128], [128, 4], [1, 128]])
                    g.dma_start(out=th_g[:, gb * 512:(gb + 1) * 512], in_=src_h).then_inc(sm_t[gb], 16)
                    src_l = aw_ap(h * 16384 + WB_LO + 128 + 512 * Kg,
                                  [[16384, 1], [-1, 128], [128, 4], [1, 128]])
                    g.dma_start(out=tl_g[:, gb * 512:(gb + 1) * 512], in_=src_l).then_inc(sm_t[gb], 16)
                em("gpsimd", _)
                gq_t[(h, Kg)] = C_T[gb].inc(32)

            for K in range(NT):
                gb = (K // 4) % 4

                def _(t, h=h, K=K, gb=gb):
                    t.wait_ge(sm_t[gb], gq_t[(h, K // 4)])
                    if K == 0:
                        if h >= 1:
                            t.wait_ge(dv, dv_mixl[h - 1])
                        else:
                            t.wait_ge(dv, dv_vlo[NT - 1])
                    ins = None
                    ksl = gb * 512 + (K % 4) * 128
                    for pi, (stat, mvoff) in enumerate(((th_g, 0), (th_g, 32768), (tl_g, 0))):
                        for (c0, c1) in conv_chunks(K):
                            j0 = c0 - 64 * K
                            stop = (pi == 2 and c0 == 64 * K and K == 8 * (c0 // 512) + 7)
                            ins = t.matmul(out=bank_ap(c0, c1), lhsT=stat[:, ksl:ksl + 128],
                                           rhs=AV[:, mvoff + h * 2048 + j0: mvoff + h * 2048 + j0 + (c1 - c0)],
                                           start=(K == 0 and pi == 0), stop=stop)
                    ins.then_inc(pe, 1)
                em("tensor", _)
                pe_convK[(h, K)] = PE.inc(1)
            pe_head[h] = pe_convK[(h, NT - 1)]
            emit_wprep_load_dma(h)

            def _(a, h=h):
                a.wait_ge(pe, pe_head[h])
                a.copy(out=AW[:, MIXH:MIXH + 1024], in_=ps_a.ap()[:, :])
                a.copy(out=AW[:, MIXH + 1024:MIXH + 2048], in_=ps_b.ap()[:, :]).then_inc(ac, 1)
            em("scalar", _)
            ac_mixh[h] = AC.inc(1)

            def _(v, h=h):
                v.wait_ge(ac, ac_mixh[h])
                v.tensor_tensor(out=AW[:, MIXL:MIXL + 1024], in0=ps_a.ap()[:, :],
                                in1=AW[:, MIXH:MIXH + 1024], op=OP.subtract)
                v.tensor_tensor(out=AW[:, MIXL + 1024:MIXL + 2048], in0=ps_b.ap()[:, :],
                                in1=AW[:, MIXH + 1024:MIXH + 2048], op=OP.subtract).then_inc(dv, 1)
            em("vector", _)
            dv_mixl[h] = DV.inc(1)

            prow = (h % 2) * 64
            for g8 in range(8):
                half = (g8 % 2) * 1024

                def _(t, h=h, g8=g8, half=half, prow=prow):
                    t.wait_ge(dv, dv_mixl[h])
                    if (h, g8 - 2) in dv_tpev:
                        t.wait_ge(dv, dv_tpev[(h, g8 - 2)])
                    elif h >= 2:
                        t.wait_ge(dv, dv_wprep_last[h - 2])
                    elif h == 1:
                        t.wait_ge(dv, dv_tpev[(0, 6 + g8)])
                    ins = None
                    for jj in range(4):
                        J = g8 * 4 + jj
                        ins = t.transpose(out=ps_tp.ap()[prow:prow + 64, half + jj * 128: half + (jj + 1) * 128],
                                          in_=AW[:, MIXH + J * 64: MIXH + (J + 1) * 64], identity=idb.ap())
                    for jj in range(4):
                        J = g8 * 4 + jj
                        ins = t.transpose(out=ps_tp.ap()[prow:prow + 64, half + 512 + jj * 128: half + 512 + (jj + 1) * 128],
                                          in_=AW[:, MIXL + J * 64: MIXL + (J + 1) * 64], identity=idb.ap())
                    ins.then_inc(pe, 1)
                em("tensor", _)
                pe_tpg[(h, g8)] = PE.inc(1)

                def _(v, h=h, g8=g8, half=half, prow=prow):
                    v.wait_ge(pe, pe_tpg[(h, g8)])
                    if (h, g8 - 2) in sq_spill:
                        v.wait_ge(sm_sp[g8 % 2], sq_spill[(h, g8 - 2)])
                    elif h >= 2:
                        v.wait_ge(sm_sp[g8 % 2], sq_spill[(h - 2, 6 + g8)])
                    stoff = TPST[g8 % 2]
                    v.tensor_copy(out=AW[prow:prow + 64, stoff:stoff + 1024],
                                  in_=ps_tp.ap()[prow:prow + 64, half:half + 1024]).then_inc(dv, 1)
                em("vector", _)
                dv_tpev[(h, g8)] = DV.inc(1)

                def _(s, h=h, g8=g8, prow=prow):
                    s.wait_ge(dv, dv_tpev[(h, g8)])
                    stoff = TPST[g8 % 2]
                    F = h // 2
                    dst_off = g8 * 4 * 128 * 1024 + prow * 1024 + F * 128
                    dims = [[1024, 64], [128 * 1024, 4], [1, 128]]
                    s.dma_start(out=bass_rust.AP(tensor=mxt_h_d.tensor, offset=dst_off, ap=dims),
                                in_=AW[prow:prow + 64, stoff:stoff + 512]).then_inc(sm_sp[g8 % 2], 16)
                    s.dma_start(out=bass_rust.AP(tensor=mxt_l_d.tensor, offset=dst_off, ap=dims),
                                in_=AW[prow:prow + 64, stoff + 512:stoff + 1024]).then_inc(sm_sp[g8 % 2], 16)
                em("sync", _)
                sq_spill[(h, g8)] = C_SP[g8 % 2].inc(32)

            if h >= 1:
                emit_wprep_store(h - 1, dv_tpev[(h, 7)])
                if h - 1 == NE - 1:
                    wp_wo_total[0] = C_WP.n
            emit_wprep_load(h)

            if h == 9:
                def _(s, wo_total=wp_wo_total[0]):
                    s.wait_ge(pe, pe_head[NE - 1])
                    s.wait_ge(sm_wp, wo_total)
                    s.dma_start(out=AV[:, 0:8192], in_=wt_res_ap(wot_h_d)).then_inc(sm_wres, 16)
                    s.dma_start(out=AV[:, 8192:16384], in_=wt_res_ap(wot_l_d)).then_inc(sm_wres, 16)
                em("sync", _)
        emit_wprep_store(H - 1, dv_tpev[(H - 1, 7)])

        BIAS = 32768

        def _(s):
            s.wait_ge(pe, pe_head[H - 1])
            s.dma_start(out=POOL[0:1, 6144:7168], in_=bo_d.unsqueeze(0)).then_inc(sm_wres, 16)
            s.dma_start(out=POOL[0:1, 7168:8192], in_=fb_d.unsqueeze(0)).then_inc(sm_wres, 16)
        em("sync", _)
        sq_wres = 64

        def _(s, wp_total=C_WP.n):
            s.wait_ge(pe, pe_head[H - 1])
            s.wait_ge(sm_wp, wp_total)
            s.dma_start(out=AV[:, 16384:24576], in_=wt_res_ap(wdt_h_d)).then_inc(sm_wres, 16)
            s.dma_start(out=AV[:, 24576:32768], in_=wt_res_ap(wdt_l_d)).then_inc(sm_wres, 16)
        em("sync", _)
        sq_wres_all = 96

        def _(a):
            a.wait_ge(sm_wres, sq_wres)
            a.copy(out=AV[0:1, BIAS:BIAS + 1024], in_=POOL[0:1, 6144:7168])
            a.copy(out=AV[0:1, BIAS + 2048:BIAS + 3072], in_=POOL[0:1, 7168:8192])
            a.drain().then_inc(ac, 1)
        em("scalar", _)
        ac_bias = AC.inc(1)

        def _(v):
            v.wait_ge(ac, ac_bias)
            v.tensor_tensor(out=AV[0:1, BIAS + 1024:BIAS + 2048], in0=POOL[0:1, 6144:7168],
                            in1=AV[0:1, BIAS:BIAS + 1024], op=OP.subtract)
            v.tensor_tensor(out=AV[0:1, BIAS + 3072:BIAS + 4096], in0=POOL[0:1, 7168:8192],
                            in1=AV[0:1, BIAS + 2048:BIAS + 3072], op=OP.subtract)
            v.drain().then_inc(dv, 1)
        em("vector", _)
        dv_bias = DV.inc(1)

        XHP4, XLP4, XDTH4, XDTL4, M1C, M2C = 4096, 6144, 8192, 10240, 12288, 14336
        Q_ABS, Q_SC, Q_INV8, Q_SC8 = 8, 72, 136, 8
        NP = NT // 2

        def mxt_stream_ap(d, T):
            return d[T * 128:(T + 1) * 128, :]

        o2pair = POOL[:, 4096:6144]
        s0p = POOL[:, 0:2048]
        s1p = POOL[:, 2048:4096]
        m1 = AW[:, M1C:M1C + 2048]
        m2 = AW[:, M2C:M2C + 2048]
        m3 = AV[:, 61440:63488]

        XHP_SET = (AW[:, XHP4:XHP4 + 2048], AV[:, 36864:38912])
        XLP_SET = (AW[:, XLP4:XLP4 + 2048], AV[:, 38912:40960])
        XDT_SET = ((AW[:, 8192:10240], AW[:, 10240:12288]),
                   (AV[:, 59392:61440], AV[:, 63488:65536]))
        Y1R = AV[:, 40960:45056].bitcast(F32)
        Y1N = [AV[:, 45056 + 4096 * g:45056 + 4096 * (g + 1)].bitcast(F32) for g in range(3)]
        GB = AV[:, 57344:59392].bitcast(F32)

        def xdt(T):
            return XDT_SET[(T // 2) % 2][T % 2]

        sq_mt = [0] * NT
        sq_x2 = [0] * NP
        pe_o = [0] * NT
        pe_xtp2 = [0] * NT
        pe_ffn = [0] * NT
        dv_tp2 = [0] * NT
        dv_add2 = [0] * NT
        dv_add1p = [0] * NP
        ac_ln1a = [0] * NP
        ac_ln1b = [0] * NP
        ac_ln2a = [0] * NP
        ac_ln2b = [0] * NP
        dq_p = [0] * NP
        sq_out = [0] * NT

        def emit_lnA(src2, smc, wait_dv):
            def _(a, src2=src2, smc=smc, wait_dv=wait_dv):
                a.wait_ge(dv, wait_dv)
                a.activation(out=GB, in_=src2[:, 0:1024], func=AF.Identity,
                             accum_out=SM[:, smc:smc + 1])
                a.activation(out=GB, in_=src2[:, 1024:2048], func=AF.Identity,
                             accum_out=SM[:, smc + 1:smc + 2])
                a.drain()
                a.activation(out=SM[:, smc + 2:smc + 4], in_=SM[:, smc:smc + 2],
                             func=AF.Identity, scale=-INV1024)
                a.drain()
                a.activation(out=GB, in_=src2[:, 0:1024], func=AF.Square,
                             bias=SM[:, smc + 2:smc + 3], accum_out=SM[:, smc + 4:smc + 5])
                a.activation(out=GB, in_=src2[:, 1024:2048], func=AF.Square,
                             bias=SM[:, smc + 3:smc + 4], accum_out=SM[:, smc + 5:smc + 6])
                a.drain()
                a.activation(out=SM[:, smc + 6:smc + 8], in_=SM[:, smc + 4:smc + 6],
                             func=AF.Sqrt, scale=INV1024, bias=1e-5)
                a.drain().then_inc(ac, 1)
            em("scalar", _)
            return AC.inc(1)

        def emit_ln_recip(smc, wait_ac):
            def _(v, smc=smc, wait_ac=wait_ac):
                v.wait_ge(ac, wait_ac)
                v.reciprocal(out=SM[:, smc + 8:smc + 10], in_=SM[:, smc + 6:smc + 8])
                v.drain().then_inc(dv, 1)
            em("vector", _)
            return DV.inc(1)

        def emit_lnB(src2, dst0, dst1, smc, wait_dv, extra=None):
            def _(a, src2=src2, dst0=dst0, dst1=dst1, smc=smc, wait_dv=wait_dv, extra=extra):
                a.wait_ge(dv, wait_dv)
                if extra is not None:
                    extra(a)
                a.activation(out=SM[:, smc + 10:smc + 11], in_=SM[:, smc + 2:smc + 3],
                             func=AF.Identity, scale=SM[:, smc + 8:smc + 9])
                a.activation(out=SM[:, smc + 11:smc + 12], in_=SM[:, smc + 3:smc + 4],
                             func=AF.Identity, scale=SM[:, smc + 9:smc + 10])
                a.drain()
                a.activation(out=dst0, in_=src2[:, 0:1024], func=AF.Identity,
                             scale=SM[:, smc + 8:smc + 9], bias=SM[:, smc + 10:smc + 11])
                a.activation(out=dst1, in_=src2[:, 1024:2048], func=AF.Identity,
                             scale=SM[:, smc + 9:smc + 10], bias=SM[:, smc + 11:smc + 12])
                a.drain().then_inc(ac, 1)
            em("scalar", _)
            return AC.inc(1)

        def quant_gen(b):
            y1n = Y1N[b % 3]
            in3 = y1n.rearrange("p (b k) -> p b k", b=64)
            s0p3 = s0p.rearrange("p (b k) -> p b k", b=64)
            xh_dst = XHP_SET[b % 2]
            xl_dst = XLP_SET[b % 2]

            def _(v, b=b):
                v.wait_ge(ac, ac_ln1b[b])
                v.tensor_reduce(out=SM[:, Q_ABS:Q_ABS + 64], in_=in3, axis=AX.X,
                                op=OP.max, apply_absolute_value=True)
                v.drain()
                v.tensor_scalar(out=SM[:, Q_SC:Q_SC + 64], in0=SM[:, Q_ABS:Q_ABS + 64],
                                scalar1=C75I, scalar2=1e-30, op0=OP.mult, op1=OP.max)
                v.drain()
                v.reciprocal(out=SM[:, Q_INV8:Q_INV8 + 64], in_=SM[:, Q_SC:Q_SC + 64])
                v.drain()
                v.tensor_scalar(out=SM[:, Q_INV8:Q_INV8 + 64], in0=SM[:, Q_INV8:Q_INV8 + 64],
                                scalar1=8.0, scalar2=None, op0=OP.mult)
                v.tensor_scalar(out=SM[:, Q_SC8:Q_SC8 + 64], in0=SM[:, Q_SC:Q_SC + 64],
                                scalar1=0.125, scalar2=None, op0=OP.mult)
                v.drain()
                inv8bc = SM[:, Q_INV8:Q_INV8 + 64].unsqueeze(2).broadcast_to([128, 64, 32])
                v.tensor_tensor(out=s0p3, in0=in3, in1=inv8bc, op=OP.mult)
                v.drain().then_inc(dv, 1)
            em("vector", _)
            d1 = DV.inc(1)

            def _(a):
                a.wait_ge(dv, d1)
                a.activation(out=s1p, in_=s0p, func=AF.Abs)
                a.drain().then_inc(ac, 1)
            em("scalar", _)
            a1 = AC.inc(1)
            yield

            def _(v):
                v.wait_ge(ac, a1)
                v.tensor_scalar(out=m1, in0=s1p, scalar1=15.5, scalar2=0.5, op0=OP.is_le, op1=OP.max)
                v.tensor_scalar(out=m2, in0=s1p, scalar1=31.0, scalar2=0.5, op0=OP.is_le, op1=OP.max)
                v.tensor_tensor(out=m3, in0=m1, in1=m2, op=OP.mult)
                v.tensor_tensor(out=s0p, in0=s0p, in1=m3, op=OP.mult)
                v.drain().then_inc(dv, 1)
            em("vector", _)
            d2 = DV.inc(1)

            def _(a):
                a.wait_ge(dv, d2)
                a.activation(out=s1p, in_=s0p, func=AF.Copy, bias=0.5, scale=-1.0)
                a.drain().then_inc(ac, 1)
            em("scalar", _)
            a2 = AC.inc(1)
            yield

            dcv = [0] * 4
            acv = [0] * 4
            for cq in range(4):
                def _(v, cq=cq, a2=a2):
                    if cq == 0:
                        v.wait_ge(ac, a2)
                    if cq >= 2:
                        v.wait_ge(ac, acv[cq - 2])
                    ib = ibuf.ap()[:, (cq % 2) * 512:(cq % 2) * 512 + 512]
                    v.tensor_scalar(out=ib, in0=s1p[:, cq * 512:(cq + 1) * 512],
                                    scalar1=0.0, scalar2=None, op0=OP.add)
                    v.drain().then_inc(dv, 1)
                em("vector", _)
                dcv[cq] = DV.inc(1)

                def _(a, cq=cq):
                    a.wait_ge(dv, dcv[cq])
                    ib = ibuf.ap()[:, (cq % 2) * 512:(cq % 2) * 512 + 512]
                    a.copy(out=s0p[:, cq * 512:(cq + 1) * 512], in_=ib)
                    a.drain().then_inc(ac, 1)
                em("scalar", _)
                acv[cq] = AC.inc(1)
            yield

            def _(v, aend=acv[3]):
                v.wait_ge(ac, aend)
                v.tensor_scalar(out=m1, in0=m1, scalar1=-4.0, scalar2=6.0, op0=OP.mult, op1=OP.add)
                v.tensor_scalar(out=m2, in0=m2, scalar1=-1.0, scalar2=1.5, op0=OP.mult, op1=OP.add)
                v.tensor_tensor(out=m1, in0=m1, in1=m2, op=OP.mult)
                v.tensor_tensor(out=m2, in0=s0p, in1=s1p, op=OP.is_gt)
                v.tensor_tensor(out=s0p, in0=m2, in1=s0p, op=OP.subtract)
                v.tensor_tensor(out=s0p, in0=s0p, in1=m1, op=OP.mult)
                sc8bc = SM[:, Q_SC8:Q_SC8 + 64].unsqueeze(2).broadcast_to([128, 64, 32])
                v.tensor_tensor(out=s0p3, in0=s0p3, in1=sc8bc, op=OP.mult)
                v.drain().then_inc(dv, 1)
            em("vector", _)
            d4 = DV.inc(1)
            yield

            def _(a, b=b, xh_dst=xh_dst):
                a.wait_ge(dv, d4)
                if b >= 2:
                    a.wait_ge(pe, pe_xtp2[2 * (b - 2) + 1])
                a.copy(out=xh_dst, in_=s0p)
                a.drain().then_inc(ac, 1)
            em("scalar", _)
            a4 = AC.inc(1)

            def _(v, xh_dst=xh_dst, xl_dst=xl_dst):
                v.wait_ge(ac, a4)
                v.tensor_tensor(out=xl_dst, in0=s0p, in1=xh_dst, op=OP.subtract)
                v.drain().then_inc(dv, 1)
            em("vector", _)
            dq_p[b] = DV.inc(1)
            yield

        def emit_xtp2_pe(T):
            off = (T % 2) * 1024
            xh_src = XHP_SET[(T // 2) % 2]
            xl_src = XLP_SET[(T // 2) % 2]

            def _(t, T=T, off=off, xh_src=xh_src, xl_src=xl_src):
                t.wait_ge(dv, dq_p[T // 2])
                if T >= 1:
                    t.wait_ge(dv, dv_tp2[T - 1])
                ins = None
                for Ei in range(NE):
                    ins = t.transpose(out=ps_tp.ap()[:, Ei * 128:(Ei + 1) * 128],
                                      in_=xh_src[:, off + Ei * 128:off + (Ei + 1) * 128], identity=idb.ap())
                for Ei in range(NE):
                    ins = t.transpose(out=ps_tp.ap()[:, 1024 + Ei * 128:1024 + (Ei + 1) * 128],
                                      in_=xl_src[:, off + Ei * 128:off + (Ei + 1) * 128], identity=idb.ap())
                ins.then_inc(pe, 1)
            em("tensor", _)
            pe_xtp2[T] = PE.inc(1)

        def emit_xtp2_evac(T):
            def _(v, T=T):
                v.wait_ge(pe, pe_xtp2[T])
                if T >= 4:
                    v.wait_ge(pe, pe_ffn[T - 4])
                v.tensor_copy(out=xdt(T), in_=ps_tp.ap()[:, 0:2048])
                v.drain().then_inc(dv, 1)
            em("vector", _)
            dv_tp2[T] = DV.inc(1)

        def ffn_ps(T):
            return ps_b if T % 2 == 0 else ps_a

        def emit_ffn_pe(T):
            fps = ffn_ps(T)

            def _(t, T=T, fps=fps):
                if T < 2:
                    t.wait_ge(sm_wres, sq_wres_all)
                t.wait_ge(dv, dv_tp2[T])
                if T % 2 == 0:
                    if T >= 2:
                        t.wait_ge(dv, dv_add2[T - 2])
                else:
                    if T // 2 + 2 <= NP - 1:
                        t.wait_ge(dv, dv_add1p[T // 2 + 2])
                    else:
                        t.wait_ge(dv, dv_add2[T - 2])
                d = xdt(T)
                ins = None
                for Ei in range(NE):
                    xh_s = d[:, Ei * 128:(Ei + 1) * 128]
                    xl_s = d[:, 1024 + Ei * 128:1024 + (Ei + 1) * 128]
                    for pi, (stat, woff) in enumerate(((xh_s, 16384), (xh_s, 24576), (xl_s, 16384))):
                        for ch in range(2):
                            ins = t.matmul(out=fps.ap()[:, ch * 512:(ch + 1) * 512],
                                           lhsT=stat,
                                           rhs=AV[:, woff + Ei * 1024 + ch * 512: woff + Ei * 1024 + (ch + 1) * 512],
                                           start=(Ei == 0 and pi == 0), stop=False)
                for bi, boff in enumerate((BIAS + 2048, BIAS + 3072)):
                    for ch in range(2):
                        ins = t.matmul(out=fps.ap()[:, ch * 512:(ch + 1) * 512],
                                       lhsT=ones_bf.ap(),
                                       rhs=AV[0:1, boff + ch * 512: boff + (ch + 1) * 512],
                                       start=False, stop=(bi == 1))
                ins.then_inc(pe, 1)
            em("tensor", _)
            pe_ffn[T] = PE.inc(1)

        def emit_ffn_add(T):
            off = (T % 2) * 1024
            p = T // 2

            def _(v, T=T, p=p, off=off):
                v.wait_ge(pe, pe_ffn[T])
                if T >= 2:
                    v.wait_ge(sm_out, sq_out[T - 2])
                if p >= 1:
                    v.wait_ge(ac, ac_ln2b[p - 1])
                v.tensor_tensor(out=pslot(4 + (T % 2)), in0=ffn_ps(T).ap()[:, :],
                                in1=Y1N[p % 3][:, off:off + 1024], op=OP.add)
                v.drain().then_inc(dv, 1)
            em("vector", _)
            dv_add2[T] = DV.inc(1)

        def emit_opj(T, vps):
            buf = T % 2

            def _(t, T=T, buf=buf, vps=vps):
                t.wait_ge(sm_mt[buf], sq_mt[T])
                if T % 2 == 0 and T >= 6:
                    t.wait_ge(dv, dv_add2[T - 5])
                elif T >= 2:
                    t.wait_ge(dv, dv_add1p[T // 2 - 1])
                else:
                    t.wait_ge(dv, dv_mixl[H - 1])
                    t.wait_ge(dv, dv_bias)
                ins = None
                for Fi in range(NE):
                    mh = AW[:, MTH[buf] + Fi * 128: MTH[buf] + (Fi + 1) * 128]
                    ml = AW[:, MTL[buf] + Fi * 128: MTL[buf] + (Fi + 1) * 128]
                    for pi, (stat, woff) in enumerate(((mh, 0), (mh, 8192), (ml, 0))):
                        for ch in range(2):
                            ins = t.matmul(out=vps.ap()[:, ch * 512:(ch + 1) * 512],
                                           lhsT=stat,
                                           rhs=AV[:, woff + Fi * 1024 + ch * 512: woff + Fi * 1024 + (ch + 1) * 512],
                                           start=(Fi == 0 and pi == 0), stop=False)
                for bi, boff in enumerate((BIAS, BIAS + 1024)):
                    for ch in range(2):
                        ins = t.matmul(out=vps.ap()[:, ch * 512:(ch + 1) * 512],
                                       lhsT=ones_bf.ap(),
                                       rhs=AV[0:1, boff + ch * 512: boff + (ch + 1) * 512],
                                       start=False, stop=(bi == 1))
                ins.then_inc(pe, 1)
            em("tensor", _)
            pe_o[T] = PE.inc(1)

        def emit_ln2_tail(p):
            t0, t1 = 2 * p, 2 * p + 1
            dv_rc2 = emit_ln_recip(200, ac_ln2a[p])

            def slot_gate(a, p=p):
                if p >= 1:
                    a.wait_ge(sm_out, sq_out[2 * p - 1])
            ac_ln2b[p] = emit_lnB(o2pair, pslot(6), pslot(7), 200, dv_rc2, extra=slot_gate)

            def _(g, t0=t0, t1=t1, w=ac_ln2b[p]):
                g.wait_ge(ac, w)
                g.dma_start(out=y_d[t0 * 128:(t0 + 1) * 128, :], in_=pslot(6)).then_inc(sm_out, 16)
                g.dma_start(out=y_d[t1 * 128:(t1 + 1) * 128, :], in_=pslot(7)).then_inc(sm_out, 16)
            em("gpsimd", _)
            sq_out[t0] = C_OUT.inc(16)
            sq_out[t1] = C_OUT.inc(16)

        for b in range(NP):
            t0, t1 = 2 * b, 2 * b + 1
            u0, u1 = 2 * (b - 1), 2 * (b - 1) + 1
            w0, w1 = 2 * (b - 2), 2 * (b - 2) + 1

            def _(s, b=b, t0=t0, t1=t1, sp0=C_SP[0].n, sp1=C_SP[1].n):
                if b == 0:
                    s.wait_ge(sm_wres, sq_wres)
                    s.wait_ge(sm_sp[0], sp0)
                    s.wait_ge(sm_sp[1], sp1)
                    s.wait_ge(dv, dv_bias)
                if t0 >= 2:
                    s.wait_ge(pe, pe_o[t0 - 2])
                s.dma_start(out=AW[:, MTH[0]:MTH[0] + 1024],
                            in_=mxt_stream_ap(mxt_h_d, t0)).then_inc(sm_mt[0], 16)
                s.dma_start(out=AW[:, MTL[0]:MTL[0] + 1024],
                            in_=mxt_stream_ap(mxt_l_d, t0)).then_inc(sm_mt[0], 16)
                if t1 >= 2:
                    s.wait_ge(pe, pe_o[t1 - 2])
                s.dma_start(out=AW[:, MTH[1]:MTH[1] + 1024],
                            in_=mxt_stream_ap(mxt_h_d, t1)).then_inc(sm_mt[1], 16)
                s.dma_start(out=AW[:, MTL[1]:MTL[1] + 1024],
                            in_=mxt_stream_ap(mxt_l_d, t1)).then_inc(sm_mt[1], 16)
            em("sync", _)
            sq_mt[t0] = C_MT[0].inc(32)
            sq_mt[t1] = C_MT[1].inc(32)

            xs0, xs1 = (4, 5) if b == 0 else (2, 3)

            def _(s, b=b, t0=t0, t1=t1, xs0=xs0, xs1=xs1):
                if b == 0:
                    s.wait_ge(dv, wprep_d1[2 * NE - 1])
                else:
                    s.wait_ge(dv, dq_p[b - 1])
                    if b == 1:
                        s.wait_ge(dv, wprep_loads[2 * NE - 1])
                s.dma_start(out=pslot(xs0), in_=x_d[t0 * 128:(t0 + 1) * 128, :]).then_inc(sm_x[0], 16)
                s.dma_start(out=pslot(xs1), in_=x_d[t1 * 128:(t1 + 1) * 128, :]).then_inc(sm_x[1], 16)
            em("sync", _)
            sq_x2[b] = (C_X[0].inc(16), C_X[1].inc(16))

            emit_opj(t0, ps_a)
            emit_opj(t1, ps_c)

            def _(v, b=b):
                v.wait_ge(pe, pe_o[2 * b + 1])
                v.wait_ge(sm_x[0], sq_x2[b][0])
                v.wait_ge(sm_x[1], sq_x2[b][1])
                if b >= 1:
                    v.wait_ge(ac, ac_ln1b[b - 1])
                v.tensor_tensor(out=Y1R[:, 0:1024], in0=ps_a.ap()[:, :],
                                in1=pslot(4 if b == 0 else 2), op=OP.add)
                v.tensor_tensor(out=Y1R[:, 1024:2048], in0=ps_c.ap()[:, :],
                                in1=pslot(5 if b == 0 else 3), op=OP.add)
                v.drain().then_inc(dv, 1)
            em("vector", _)
            dv_add1p[b] = DV.inc(1)

            ac_ln1a[b] = emit_lnA(Y1R, 216, dv_add1p[b])

            if b >= 1:
                emit_xtp2_pe(u0)
            if b >= 2:
                emit_ffn_pe(w0)

            dv_rc1 = emit_ln_recip(216, ac_ln1a[b])

            if b >= 1:
                emit_xtp2_evac(u0)

            def y1n_gate(a, b=b):
                if b >= 3:
                    a.wait_ge(dv, dv_add2[2 * (b - 3) + 1])
            ac_ln1b[b] = emit_lnB(Y1R, Y1N[b % 3][:, 0:1024], Y1N[b % 3][:, 1024:2048],
                                  216, dv_rc1, extra=y1n_gate)

            qg = quant_gen(b)
            next(qg)
            if b >= 2:
                emit_ffn_add(w0)
            if b >= 1:
                emit_xtp2_pe(u1)
            next(qg)
            if b >= 1:
                emit_xtp2_evac(u1)
            next(qg)
            if b >= 2:
                emit_ffn_pe(w1)
                emit_ffn_add(w1)
            next(qg)
            if b >= 2:
                ac_ln2a[b - 2] = emit_lnA(o2pair, 200, dv_add2[w1])
            next(qg)
            if b >= 2:
                emit_ln2_tail(b - 2)

        emit_xtp2_pe(2 * NP - 2)
        emit_ffn_pe(2 * NP - 4)
        emit_xtp2_evac(2 * NP - 2)
        emit_ffn_add(2 * NP - 4)
        emit_xtp2_pe(2 * NP - 1)
        emit_ffn_pe(2 * NP - 3)
        emit_xtp2_evac(2 * NP - 1)
        emit_ffn_add(2 * NP - 3)
        ac_ln2a[NP - 2] = emit_lnA(o2pair, 200, dv_add2[2 * NP - 3])
        emit_ln2_tail(NP - 2)
        emit_ffn_pe(2 * NP - 2)
        emit_ffn_add(2 * NP - 2)
        emit_ffn_pe(2 * NP - 1)
        emit_ffn_add(2 * NP - 1)
        ac_ln2a[NP - 1] = emit_lnA(o2pair, 200, dv_add2[2 * NP - 1])
        emit_ln2_tail(NP - 1)

        def _(s, total=C_OUT.n):
            s.wait_ge(sm_out, total)
        em("sync", _)

        if debug_taps:
            def _(s):
                s.dma_start(out=dbg_res, in_=AV[:, 0:32768]).then_inc(sm_in, 16)
                s.dma_start(out=dbg_th[:, 0:128], in_=th_bufs[1].ap()[:, 384:512]).then_inc(sm_in, 16)
                s.dma_start(out=dbg_mh, in_=AW[:, MIXH:MIXH + 2048]).then_inc(sm_in, 16)
                s.dma_start(out=dbg_ml, in_=AW[:, MIXL:MIXL + 2048]).then_inc(sm_in, 16)
            em("sync", _)

        @block.sync
        def _(eng):
            for fn in ops["sync"]:
                fn(eng)

        @block.gpsimd
        def _(eng):
            for fn in ops["gpsimd"]:
                fn(eng)

        @block.vector
        def _(eng):
            for fn in ops["vector"]:
                fn(eng)

        @block.scalar
        def _(eng):
            for fn in ops["scalar"]:
                fn(eng)

        @block.tensor
        def _(eng):
            for fn in ops["tensor"]:
                fn(eng)

    return nc


_NC_CACHE = None


def kernel(L0_attn_x, L0_attn_W_A, L0_attn_W_V, L0_attn_W_O, L0_attn_b_O, L0_ffn_weight, L0_ffn_bias):
    global _NC_CACHE
    x = np.ascontiguousarray(np.asarray(L0_attn_x, dtype=np.float32))
    wa = np.ascontiguousarray(np.asarray(L0_attn_W_A, dtype=np.float32))
    wv = np.ascontiguousarray(np.asarray(L0_attn_W_V, dtype=np.float32))
    wo = np.ascontiguousarray(np.asarray(L0_attn_W_O, dtype=np.float32))
    bo = np.ascontiguousarray(np.asarray(L0_attn_b_O, dtype=np.float32))
    fw = np.ascontiguousarray(np.asarray(L0_ffn_weight, dtype=np.float32))
    fb = np.ascontiguousarray(np.asarray(L0_ffn_bias, dtype=np.float32))

    if _NC_CACHE is None:
        _NC_CACHE = build_nc()
    nc = _NC_CACHE

    in_maps = [dict(x=x[b], wa=wa, wv=wv, wo=wo, bo=bo, fw=fw, fb=fb) for b in range(8)]
    res = run_bass_kernel_spmd(nc, in_maps, core_ids=list(range(8)))
    out = np.stack([res.results[b]["y"] for b in range(8)], axis=0)
    return out


if __name__ == "__main__":
    rng = np.random.default_rng(0)
    ins = dict(
        L0_attn_x=rng.standard_normal((8, S, E)).astype(np.float32),
        L0_attn_W_A=(rng.standard_normal((H, E)) * E ** -0.5).astype(np.float32),
        L0_attn_W_V=(rng.standard_normal((E, E)) * E ** -0.5).astype(np.float32),
        L0_attn_W_O=(rng.standard_normal((E, E)) * E ** -0.5).astype(np.float32),
        L0_attn_b_O=(rng.standard_normal((E,)) * 0.01).astype(np.float32),
        L0_ffn_weight=(rng.standard_normal((E, E)) * 0.02).astype(np.float32),
        L0_ffn_bias=(rng.standard_normal((E,)) * 0.02).astype(np.float32),
    )
    out = kernel(**ins)
    print("out", out.shape, out.dtype, float(np.abs(out).mean()))
